# version 04
# baseline (speedup 1.0000x reference)
import numpy as np
import ml_dtypes
import concourse.bass as bass
import concourse.mybir as mybir
from concourse.bass_utils import run_bass_kernel_spmd

F32 = mybir.dt.float32
BF16 = mybir.dt.bfloat16
I32 = mybir.dt.int32
AF = mybir.ActivationFunctionType
ALU = mybir.AluOpType
AXX = mybir.AxisListType.X
BF = ml_dtypes.bfloat16
ts = bass.ts

D = 2048
TPC = 1024
NCH = 16
HF = 5632
NHC = 44
RS2 = 0.08838834764831845


class Sem:
    def __init__(self, h):
        self.h = h
        self.n = 0

    def inc(self, instr, k=1):
        instr.then_inc(self.h, k)
        self.n += k
        return self.n

    def wait(self, eng, n=None):
        eng.wait_ge(self.h, self.n if n is None else n)


class Prog:
    def __init__(self):
        self.nc = bass.Bass("TRN2", target_bir_lowering=False, debug=False)
        self._cms = {}
        self._arena = None
        self._frees = []
        self._allocs = {}

    def arena_init(self, words):
        self._arena = self.sbuf("arena", [128, words], F32)
        self._frees = [(0, words)]

    def aa(self, name, n, dt):
        esz = 2 if dt in (BF16, mybir.dt.float16) else 4
        w4 = (n * esz + 3) // 4
        for i, (a, b) in enumerate(self._frees):
            if b - a >= w4:
                self._frees[i] = (a + w4, b)
                if self._frees[i][0] == self._frees[i][1]:
                    self._frees.pop(i)
                self._allocs[name] = (a, a + w4)
                v = self._arena[:, a:a + w4]
                return v if dt is F32 else v.bitcast(dt)
        raise MemoryError(f"arena: no room for {name} ({w4} words); frees={self._frees}")

    def af(self, *names):
        for name in names:
            a, b = self._allocs.pop(name)
            self._frees.append((a, b))
        self._frees.sort()
        m = []
        for a, b in self._frees:
            if m and m[-1][1] == a:
                m[-1] = (m[-1][0], b)
            else:
                m.append((a, b))
        self._frees = [tuple(x) for x in m]

    def dram_in(self, name, shape, dt):
        return self.nc.dram_tensor(name, list(shape), dt, kind="ExternalInput").ap()

    def dram_out(self, name, shape, dt):
        return self.nc.dram_tensor(name, list(shape), dt, kind="ExternalOutput").ap()

    def dram_tmp(self, name, shape, dt):
        return self.nc.dram_tensor(name, list(shape), dt, kind="Internal")

    def sbuf(self, name, shape, dt):
        cm = self.nc.sbuf_tensor(name, list(shape), dt)
        t = cm.__enter__()
        self._cms[name] = cm
        return t

    def psum(self, name, shape, dt=F32):
        cm = self.nc.psum_tensor(name, list(shape), dt)
        t = cm.__enter__()
        self._cms[name] = cm
        return t

    def free(self, *names):
        for n in names:
            self._cms.pop(n).__exit__(None, None, None)

    def sem(self, name):
        cm = self.nc.semaphore(name)
        h = cm.__enter__()
        self._cms["sem_" + name] = cm
        return Sem(h)

    def begin(self):
        self._blk_cm = self.nc.Block()
        self.blk = self._blk_cm.__enter__()
        return self.blk

    def end(self):
        self._blk_cm.__exit__(None, None, None)
        for n in reversed(list(self._cms)):
            self._cms.pop(n).__exit__(None, None, None)
        return self.nc


def common_setup(P, blk):
    P.dma = P.sem("dma")
    P.dmg = P.sem("dmg")
    P.pe = P.sem("pe")
    P.dv = P.sem("dv")
    P.sc = P.sem("sc")
    P.gp = P.sem("gp")
    P.ones128_f = P.sbuf("ones128_f", [128, 1], F32)
    P.ones128_b = P.sbuf("ones128_b", [128, 1], BF16)
    P.ones1_f = P.sbuf("ones1_f", [1, 128], F32)
    P.ones1_b = P.sbuf("ones1_b", [1, 128], BF16)
    iw = P.sbuf("iwork", [128, 128], I32)
    P.iw = iw
    P.ident_b = P.sbuf("ident_b", [128, 128], BF16)
    P.ident_f = P.sbuf("ident_f", [128, 128], F32)
    P.eps5 = P.sbuf("eps5", [128, 1], F32)

    @blk.gpsimd
    def _(g):
        g.memset(P.ones128_f[:], 1.0)
        g.memset(P.ones128_b[:], 1.0)
        g.memset(P.ones1_f[:], 1.0)
        g.memset(P.ones1_b[:], 1.0)
        g.memset(P.eps5[:], 1e-5)
        P.gp.inc(g.iota(iw[:], pattern=[[-1, 128]], base=0, channel_multiplier=1))

    @blk.vector
    def _(v):
        P.gp.wait(v)
        v.tensor_scalar(out=P.ident_b[:], in0=iw[:], scalar1=0, scalar2=None, op0=ALU.is_equal)
        P.dv.inc(v.tensor_scalar(out=P.ident_f[:], in0=iw[:], scalar1=0, scalar2=None, op0=ALU.is_equal))
    P.consts_dv = P.dv.n


def barrier(P, blk, engines="tvsg"):
    marks = (P.pe.n, P.dv.n, P.sc.n, P.gp.n, P.dma.n, P.dmg.n)

    def wait_all(eng):
        eng.wait_ge(P.pe.h, marks[0])
        eng.wait_ge(P.dv.h, marks[1])
        eng.wait_ge(P.sc.h, marks[2])
        eng.wait_ge(P.gp.h, marks[3])
        eng.wait_ge(P.dma.h, marks[4])
        eng.wait_ge(P.dmg.h, marks[5])

    for sec in (blk.tensor, blk.vector, blk.scalar, blk.gpsimd, blk.sync):
        @sec
        def _(e):
            wait_all(e)



def emit_ln(P, blk, ps, raw, nch, T, invn, out_cb, extra_rstd_scale=None,
            ready_pe=None):
    nt = T // 512
    sq = P.lnsq
    srow = P.lnsrow
    brow = P.lnbrow
    vrow = P.lnvrow
    rrow = P.lnrrow
    mnB = P.lnmnB
    rstdB = P.lnrstdB
    tmp = P.lntmp

    @blk.tensor
    def _(t):
        for c in range(nch):
            for tt in range(nt):
                mm = t.matmul(ps[tt][0:1, :], P.ones128_b[:, 0:1], raw(c)[:, ts(tt, 512)],
                              start=(c == 0), stop=(c == nch - 1))
                if c == nch - 1 and tt == nt - 1:
                    P.pe.inc(mm)

    sc0 = P.sc.n
    pe_sq0 = P.pe.n

    @blk.scalar
    def _(s):
        k = 0
        for c in range(nch):
            for tt in range(nt):
                if k >= 2:
                    s.wait_ge(P.pe.h, pe_sq0 + k - 1)
                P.sc.inc(s.activation(out=sq[:, ts(k % 2, 512)], in_=raw(c)[:, ts(tt, 512)],
                                      func=AF.Square))
                k += 1

    @blk.tensor
    def _(t):
        k = 0
        for c in range(nch):
            for tt in range(nt):
                t.wait_ge(P.sc.h, sc0 + k + 1)
                P.pe.inc(t.matmul(ps[2 + tt][0:1, :], P.ones128_b[:, 0:1], sq[:, ts(k % 2, 512)],
                                  start=(c == 0), stop=(c == nch - 1)))
                k += 1
    sums_pe = P.pe.n

    @blk.vector
    def _(v):
        v.wait_ge(P.pe.h, sums_pe)
        for tt in range(nt):
            v.tensor_scalar(out=srow[0:1, ts(tt, 512)], in0=ps[tt][0:1, :],
                            scalar1=invn, scalar2=None, op0=ALU.mult)
            v.tensor_scalar(out=srow[0:1, T + tt * 512:T + tt * 512 + 512],
                            in0=ps[2 + tt][0:1, :], scalar1=invn, scalar2=None, op0=ALU.mult)
        v.tensor_tensor(out=vrow[0:1, 0:T], in0=srow[0:1, 0:T], in1=srow[0:1, 0:T], op=ALU.mult)
        P.dv.inc(v.tensor_tensor(out=vrow[0:1, 0:T], in0=srow[0:1, T:2 * T],
                                 in1=vrow[0:1, 0:T], op=ALU.subtract))
    stats_dv = P.dv.n

    @blk.scalar
    def _(s):
        s.wait_ge(P.dv.h, stats_dv)
        s.activation(out=rrow[0:1, 0:T], in_=vrow[0:1, 0:T], func=AF.Ln, bias=P.eps5[0:1, 0:1])
        P.sc.inc(s.activation(out=rrow[0:1, 0:T], in_=rrow[0:1, 0:T], func=AF.Exp,
                              scale=(-0.5 if extra_rstd_scale is None else None) or -0.5))
    rstd_sc = P.sc.n

    @blk.vector
    def _(v):
        v.wait_ge(P.sc.h, rstd_sc)
        if extra_rstd_scale is not None:
            v.tensor_scalar(out=rrow[0:1, 0:T], in0=rrow[0:1, 0:T],
                            scalar1=extra_rstd_scale, scalar2=None, op0=ALU.mult)
        v.tensor_copy(out=brow[0:1, 0:T], in_=srow[0:1, 0:T])
        v.tensor_copy(out=brow[0:1, T:2 * T], in_=rrow[0:1, 0:T])
        v.tensor_tensor(out=brow[0:1, 2 * T:3 * T], in0=srow[0:1, 0:T],
                        in1=brow[0:1, 0:T], op=ALU.subtract)
        P.dv.inc(v.tensor_tensor(out=brow[0:1, 3 * T:4 * T], in0=rrow[0:1, 0:T],
                                 in1=brow[0:1, T:2 * T], op=ALU.subtract))
    rows_dv = P.dv.n

    @blk.tensor
    def _(t):
        t.wait_ge(P.dv.h, rows_dv)
        for tt in range(nt):
            t.matmul(ps[4 + tt][:, :], P.ones1_b[0:1, :], brow[0:1, ts(tt, 512)],
                     start=True, stop=False)
            t.matmul(ps[4 + tt][:, :], P.ones1_b[0:1, :],
                     brow[0:1, 2 * T + tt * 512:2 * T + tt * 512 + 512],
                     start=False, stop=True)
            t.matmul(ps[6 + tt][:, :], P.ones1_b[0:1, :],
                     brow[0:1, T + tt * 512:T + tt * 512 + 512],
                     start=True, stop=False)
            mm = t.matmul(ps[6 + tt][:, :], P.ones1_b[0:1, :],
                          brow[0:1, 3 * T + tt * 512:3 * T + tt * 512 + 512],
                          start=False, stop=True)
            if tt == nt - 1:
                P.pe.inc(mm)
    bc_pe = P.pe.n

    @blk.vector
    def _(v):
        v.wait_ge(P.pe.h, bc_pe)
        for tt in range(nt):
            v.tensor_copy(out=mnB[:, ts(tt, 512)], in_=ps[4 + tt][:, :])
            v.tensor_copy(out=rstdB[:, ts(tt, 512)], in_=ps[6 + tt][:, :])
        for c in range(nch):
            v.tensor_tensor(out=tmp[:, 0:T], in0=raw(c)[:, 0:T], in1=mnB[:, 0:T], op=ALU.subtract)
            last = out_cb(c, v, tmp, rstdB)
            if c == nch - 1:
                P.dv.inc(last)
    return P.dv.n


def alloc_ln_bufs(P, T=TPC):
    P.lnsq = P.aa("lnsq", 1024, BF16)
    P.lnbrow = P.aa("lnbrow", 4 * T, BF16)
    P.lnsrow = P.aa("lnsrow", 2 * T, F32)
    P.lnvrow = P.aa("lnvrow", T, F32)
    P.lnrrow = P.aa("lnrrow", T, F32)
    P.lnmnB = P.aa("lnmnB", T, F32)
    P.lnrstdB = P.aa("lnrstdB", T, F32)
    P.lntmp = P.aa("lntmp", T, F32)


def free_ln_bufs(P):
    P.af("lnsq", "lnbrow", "lnsrow", "lnvrow", "lnrrow", "lnmnB", "lnrstdB", "lntmp")



def build_l0_main(stage=99, dbg=False):
    P = Prog()
    x_d = P.dram_in("x", [D, TPC], F32)
    wdqkv_d = P.dram_in("wdqkv", [D, 1024], F32)
    lnp_d = P.dram_in("lnp", [128, 16], F32)
    wukv_d = P.dram_in("wukv", [512, 4096], F32)
    w1t_d = P.dram_in("w1t", [D, HF], F32)
    w2t_d = P.dram_in("w2t", [D, HF], F32)
    w3t_d = P.dram_in("w3t", [HF, D], F32)

    cq_o = P.dram_out("cq", [512, TPC], BF16)
    k_o = P.dram_out("kown", [D, TPC], BF16)
    v_o = P.dram_out("vown", [TPC, D], BF16)
    ffn_o = P.dram_out("ffn0", [D, TPC], F32)
    if dbg:
        thr_o = P.dram_out("thr_dbg", [128, 6], F32)
        hid_o = P.dram_out("hid_dbg", [128, NHC * TPC], BF16)
        cols_o = P.dram_out("cols_dbg", [128, 80], F32)
        trow_o = P.dram_out("trow_dbg", [1, 12], F32)

    blk = P.begin()
    common_setup(P, blk)
    P.arena_init(51500)
    dma, dmg, pe, dv, sc, gp = P.dma, P.dmg, P.pe, P.dv, P.sc, P.gp
    ps = [P.psum("ps%d" % i, [128, 512], F32) for i in range(8)]

    x_fm = P.aa("x_fm", NCH * TPC, BF16)
    wdqkv = P.aa("wdqkv_s", NCH * 1024, BF16)
    lnp = P.sbuf("lnp_s", [128, 16], F32)
    wukv = P.aa("wukv_s", 4 * 4096, BF16)

    @blk.sync
    def _(s):
        s.dma_start(out=lnp[:, :], in_=lnp_d[:, :]).then_inc(dma.h, 16)
    dma.n += 16

    @blk.gpsimd
    def _(g):
        g.dma_start(out=x_fm.rearrange("p (c t) -> p c t", c=NCH),
                    in_=x_d.rearrange("(c p) t -> p c t", p=128)).then_inc(dmg.h, 16)
        g.dma_start(out=wdqkv.rearrange("p (c n) -> p c n", c=NCH),
                    in_=wdqkv_d.rearrange("(c p) n -> p c n", p=128)).then_inc(dmg.h, 16)
        g.dma_start(out=wukv.rearrange("p (c n) -> p c n", c=4),
                    in_=wukv_d.rearrange("(c p) n -> p c n", p=128)).then_inc(dmg.h, 16)
    dmg.n += 48
    w_ld = dmg.n

    @blk.tensor
    def _(t):
        t.wait_ge(dma.h, dma.n)
        t.wait_ge(dmg.h, dmg.n)
        t.wait_ge(P.dv.h, P.consts_dv)
        pe.inc(t.nop())
    @blk.scalar
    def _(s):
        s.wait_ge(dmg.h, dmg.n)
        sc.inc(s.nop())

    alloc_ln_bufs(P)
    xn = P.aa("xn", NCH * TPC, BF16)

    def preln_out(c, v, tmp, rstdB):
        return v.tensor_tensor(out=xn[:, ts(c, TPC)], in0=tmp[:, 0:TPC], in1=rstdB[:, 0:TPC],
                               op=ALU.mult)

    emit_ln(P, blk, ps, lambda c: x_fm[:, ts(c, TPC)], NCH, TPC, 1.0 / D, preln_out)
    barrier(P, blk)
    P.af("x_fm")
    if stage < 2:
        @blk.sync
        def _(s):
            s.wait_ge(dma.h, dma.n)
        P.end()
        return P

    cqkv = P.aa("cqkv", 8 * TPC, BF16)
    pe0 = pe.n
    dv0 = dv.n

    @blk.tensor
    def _(t):
        t.wait_ge(dmg.h, w_ld)
        for i in range(16):
            oc, tt = i // 2, i % 2
            if i >= 4:
                t.wait_ge(dv.h, dv0 + i - 3)
            for kc in range(NCH):
                mm = t.matmul(ps[i % 4][:, :], wdqkv[:, kc * 1024 + oc * 128:kc * 1024 + oc * 128 + 128],
                              xn[:, kc * TPC + tt * 512:kc * TPC + tt * 512 + 512],
                              start=(kc == 0), stop=(kc == NCH - 1))
            pe.inc(mm)

    @blk.vector
    def _(v):
        for i in range(16):
            oc, tt = i // 2, i % 2
            v.wait_ge(pe.h, pe0 + i + 1)
            dv.inc(v.tensor_copy(out=cqkv[:, oc * TPC + tt * 512:oc * TPC + tt * 512 + 512],
                                 in_=ps[i % 4][:, :]))
    barrier(P, blk)
    P.af("wdqkv_s")

    cq = P.aa("cq_s", 4 * TPC, BF16)
    ckv = P.aa("ckv_s", 4 * TPC, BF16)
    bq_s = P.sbuf("bq_s", [128, 4], F32)

    @blk.vector
    def _(v):
        dv.inc(v.tensor_scalar(out=bq_s[:, :], in0=lnp[:, 4:8], scalar1=RS2, scalar2=None,
                               op0=ALU.mult))

    def cq_out(c, v, tmp, rstdB):
        v.tensor_tensor(out=tmp[:, 0:TPC], in0=tmp[:, 0:TPC], in1=rstdB[:, 0:TPC], op=ALU.mult)
        return v.tensor_scalar(out=cq[:, ts(c, TPC)], in0=tmp[:, 0:TPC],
                               scalar1=lnp[:, c:c + 1], scalar2=bq_s[:, c:c + 1],
                               op0=ALU.mult, op1=ALU.add)

    def ckv_out(c, v, tmp, rstdB):
        v.tensor_tensor(out=tmp[:, 0:TPC], in0=tmp[:, 0:TPC], in1=rstdB[:, 0:TPC], op=ALU.mult)
        return v.tensor_scalar(out=ckv[:, ts(c, TPC)], in0=tmp[:, 0:TPC],
                               scalar1=lnp[:, 8 + c:9 + c], scalar2=lnp[:, 12 + c:13 + c],
                               op0=ALU.mult, op1=ALU.add)

    emit_ln(P, blk, ps, lambda c: cqkv[:, ts(c, TPC)], 4, TPC, 1.0 / 512, cq_out,
            extra_rstd_scale=RS2)
    barrier(P, blk)
    if stage < 3:
        @blk.sync
        def _(s):
            s.wait_ge(dma.h, dma.n)
        P.end()
        return P
    emit_ln(P, blk, ps, lambda c: cqkv[:, ts(4 + c, TPC)], 4, TPC, 1.0 / 512, ckv_out)
    barrier(P, blk)
    P.af("cqkv")
    cq_dv = dv.n

    @blk.sync
    def _(s):
        s.dma_start(out=cq_o.rearrange("(c p) t -> p c t", p=128),
                    in_=cq.rearrange("p (c t) -> p c t", c=4)).then_inc(dma.h, 16)
    dma.n += 16

    k_own = P.aa("k_own", NCH * TPC, BF16)
    v_own = P.aa("v_own", 8 * D, BF16)
    pe0 = pe.n
    dv0 = dv.n

    @blk.tensor
    def _(t):
        for i in range(32):
            oc, tt = i // 2, i % 2
            if i >= 4:
                t.wait_ge(dv.h, dv0 + i - 3)
            for kc in range(4):
                mm = t.matmul(ps[i % 4][:, :], wukv[:, kc * 4096 + oc * 128:kc * 4096 + oc * 128 + 128],
                              ckv[:, kc * TPC + tt * 512:kc * TPC + tt * 512 + 512],
                              start=(kc == 0), stop=(kc == 3))
            pe.inc(mm)

    @blk.vector
    def _(v):
        for i in range(32):
            oc, tt = i // 2, i % 2
            v.wait_ge(pe.h, pe0 + i + 1)
            dv.inc(v.tensor_copy(out=k_own[:, oc * TPC + tt * 512:oc * TPC + tt * 512 + 512],
                                 in_=ps[i % 4][:, :]))
    k_dv = dv.n
    pe1 = pe.n
    dv1 = dv.n

    @blk.tensor
    def _(t):
        for i in range(32):
            tc, nt = i // 4, i % 4
            if i >= 4:
                t.wait_ge(dv.h, dv1 + i - 3)
            for kc in range(4):
                mm = t.matmul(ps[4 + i % 4][:, :], ckv[:, kc * TPC + tc * 128:kc * TPC + tc * 128 + 128],
                              wukv[:, kc * 4096 + 2048 + nt * 512:kc * 4096 + 2048 + nt * 512 + 512],
                              start=(kc == 0), stop=(kc == 3))
            pe.inc(mm)

    @blk.vector
    def _(v):
        for i in range(32):
            tc, nt = i // 4, i % 4
            v.wait_ge(pe.h, pe1 + i + 1)
            dv.inc(v.tensor_copy(out=v_own[:, tc * D + nt * 512:tc * D + nt * 512 + 512],
                                 in_=ps[4 + i % 4][:, :]))
    v_dv = dv.n

    @blk.sync
    def _(s):
        s.wait_ge(dv.h, k_dv)
        s.dma_start(out=k_o.rearrange("(c p) t -> p c t", p=128),
                    in_=k_own.rearrange("p (c t) -> p c t", c=NCH)).then_inc(dma.h, 16)
        s.wait_ge(dv.h, v_dv)
        s.dma_start(out=v_o.rearrange("(c p) f -> p c f", p=128),
                    in_=v_own.rearrange("p (c f) -> p c f", c=8)).then_inc(dma.h, 16)
    dma.n += 32

    barrier(P, blk)
    free_ln_bufs(P)
    P.af("k_own", "v_own", "cq_s", "ckv_s", "wukv_s")

    if stage < 5:
        @blk.sync
        def _(s):
            s.wait_ge(dma.h, dma.n)
        @blk.gpsimd
        def _(g):
            g.wait_ge(dmg.h, dmg.n)
        P.end()
        return P
    wbuf = [P.aa("wbuf%d" % i, HF, F32) for i in range(2)]
    cols = P.sbuf("cols", [128, 80], F32)
    tot = P.sbuf("tot", [128, 3], F32)
    trow = P.sbuf("trow", [1, 12], F32)
    trow_hi = P.sbuf("trow_hi", [1, 8], BF16)
    trow_lo = P.sbuf("trow_lo", [1, 8], BF16)
    thr = P.sbuf("thr", [128, 6], F32)

    streams = ([("w1", w1t_d, c, HF, c) for c in range(16)] +
               [("w2", w2t_d, c, HF, 16 + c) for c in range(16)] +
               [("w3", w3t_d, c, D, 32 + c) for c in range(44)])
    dma0 = dma.n
    dv0 = dv.n

    @blk.sync
    def _(s):
        for i, (_, src, c, width, col) in enumerate(streams):
            if i >= 2:
                s.wait_ge(dv.h, dv0 + i - 1)
            s.dma_start(out=wbuf[i % 2][:, 0:width],
                        in_=src[c * 128:(c + 1) * 128, :]).then_inc(dma.h, 16)
    dma.n += 16 * len(streams)

    @blk.vector
    def _(v):
        for i, (_, src, c, width, col) in enumerate(streams):
            v.wait_ge(dma.h, dma0 + 16 * (i + 1))
            dv.inc(v.tensor_reduce(out=cols[:, col:col + 1], in_=wbuf[i % 2][:, 0:width],
                                   axis=AXX, op=ALU.add, apply_absolute_value=True))
        v.tensor_reduce(out=tot[:, 0:1], in_=cols[:, 0:16], axis=AXX, op=ALU.add)
        v.tensor_reduce(out=tot[:, 1:2], in_=cols[:, 16:32], axis=AXX, op=ALU.add)
        dv.inc(v.tensor_reduce(out=tot[:, 2:3], in_=cols[:, 32:76], axis=AXX, op=ALU.add))
    red_dv = dv.n

    @blk.tensor
    def _(t):
        t.wait_ge(dv.h, red_dv)
        for m in range(3):
            mm = t.matmul(ps[0][0:1, m:m + 1], tot[:, m:m + 1], P.ones128_f[:, 0:1],
                          start=True, stop=True)
        pe.inc(mm)
    tot_pe = pe.n

    @blk.vector
    def _(v):
        v.wait_ge(pe.h, tot_pe)
        invn = 1.0 / (HF * D)
        v.tensor_scalar(out=trow[0:1, 0:3], in0=ps[0][0:1, 0:3], scalar1=invn * 0.5,
                        scalar2=0.5e-8, op0=ALU.mult, op1=ALU.add)
        v.tensor_scalar(out=trow[0:1, 3:6], in0=ps[0][0:1, 0:3], scalar1=-invn * 0.5,
                        scalar2=-0.5e-8, op0=ALU.mult, op1=ALU.add)
        v.tensor_scalar(out=trow_hi[0:1, 0:3], in0=ps[0][0:1, 0:3], scalar1=invn * 0.5,
                        scalar2=0.5e-8, op0=ALU.mult, op1=ALU.add)
        dv.inc(v.tensor_scalar(out=trow_hi[0:1, 3:6], in0=ps[0][0:1, 0:3],
                               scalar1=-invn * 0.5, scalar2=-0.5e-8,
                               op0=ALU.mult, op1=ALU.add))
    thr_f_dv = dv.n

    @blk.gpsimd
    def _(g):
        g.wait_ge(dv.h, thr_f_dv)
        gp.inc(g.tensor_tensor(out=trow_lo[0:1, 0:6], in0=trow[0:1, 0:6],
                               in1=trow_hi[0:1, 0:6], op=ALU.subtract))
    thr_gp = gp.n
    thr_dv0 = dv.n

    @blk.tensor
    def _(t):
        t.wait_ge(dv.h, thr_dv0)
        t.wait_ge(gp.h, thr_gp)
        t.matmul(ps[1][:, 0:6], P.ones1_b[0:1, :], trow_hi[0:1, 0:6], start=True, stop=False)
        pe.inc(t.matmul(ps[1][:, 0:6], P.ones1_b[0:1, :], trow_lo[0:1, 0:6], start=False, stop=True))
    thr_pe = pe.n

    @blk.vector
    def _(v):
        v.wait_ge(pe.h, thr_pe)
        dv.inc(v.tensor_copy(out=thr[:, 0:6], in_=ps[1][:, 0:6]))
    barrier(P, blk)
    P.af("wbuf0", "wbuf1")

    if stage < 6:
        @blk.sync
        def _(s):
            s.wait_ge(dma.h, dma.n)
        @blk.gpsimd
        def _(g):
            g.wait_ge(dmg.h, dmg.n)
        P.end()
        return P
    hid = P.aa("hid", NHC * TPC, BF16)
    cbw = [P.aa("cbw%d" % i, 2048, F32) for i in range(2)]
    sgn = [P.aa("sgn%d" % i, 2816, BF16) for i in range(4)]
    qq = [P.aa("qq%d" % i, 2048, BF16) for i in range(2)]
    tanhb = P.aa("tanhb", TPC, BF16)
    x2p = P.aa("x2p", TPC, BF16)

    dma0 = dma.n
    sc0 = sc.n
    dv0 = dv.n
    pe0 = pe.n

    @blk.sync
    def _(g):
        for hc in range(NHC):
            if hc >= 1:
                g.wait_ge(sc.h, sc0 + 4 * hc - 2)
            g.dma_start(out=cbw[0].rearrange("p (c h) -> p c h", c=NCH),
                        in_=w1t_d[:, ts(hc, 128)].rearrange("(c p) h -> p c h", p=128)
                        ).then_inc(dma.h, 16)
            g.dma_start(out=cbw[1].rearrange("p (c h) -> p c h", c=NCH),
                        in_=w2t_d[:, ts(hc, 128)].rearrange("(c p) h -> p c h", p=128)
                        ).then_inc(dma.h, 16)
    dma.n += 32 * NHC

    @blk.scalar
    def _(s):
        for hc in range(NHC):
            s.wait_ge(dma.h, dma0 + 32 * hc + 32)
            if hc >= 1:
                s.wait_ge(dv.h, dv0 + 5 * hc - 3)
            s.activation(out=sgn[0][:, 0:2048], in_=cbw[0][:, :], func=AF.Sign, bias=thr[:, 3:4])
            P.sc.inc(s.activation(out=sgn[1][:, 0:2048], in_=cbw[0][:, :], func=AF.Sign,
                                  bias=thr[:, 0:1]))
            s.activation(out=sgn[2][:, 0:2048], in_=cbw[1][:, :], func=AF.Sign, bias=thr[:, 4:5])
            P.sc.inc(s.activation(out=sgn[3][:, 0:2048], in_=cbw[1][:, :], func=AF.Sign,
                                  bias=thr[:, 1:2]))
            s.wait_ge(pe.h, pe0 + 4 * hc + 2)
            if hc >= 1:
                s.wait_ge(dv.h, dv0 + 5 * hc)
            for tt in range(2):
                P.sc.inc(s.activation(out=tanhb[:, ts(tt, 512)],
                                      in_=ps[tt][:, :], func=AF.Tanh, scale=0.25))

    @blk.vector
    def _(v):
        for hc in range(NHC):
            v.wait_ge(sc.h, sc0 + 4 * hc + 1)
            if hc >= 1:
                v.wait_ge(pe.h, pe0 + 4 * hc - 2)
            dv.inc(v.tensor_tensor(out=qq[0][:, :], in0=sgn[0][:, 0:2048], in1=sgn[1][:, 0:2048],
                                   op=ALU.add))
            v.wait_ge(sc.h, sc0 + 4 * hc + 2)
            if hc >= 1:
                v.wait_ge(pe.h, pe0 + 4 * hc)
            dv.inc(v.tensor_tensor(out=qq[1][:, :], in0=sgn[2][:, 0:2048], in1=sgn[3][:, 0:2048],
                                   op=ALU.add))
            v.wait_ge(pe.h, pe0 + 4 * hc + 4)
            v.wait_ge(sc.h, sc0 + 4 * hc + 4)
            for tt in range(2):
                dv.inc(v.tensor_scalar(out=x2p[:, ts(tt, 512)], in0=ps[2 + tt][:, :],
                                       scalar1=0.25, scalar2=None, op0=ALU.mult))
            dv.inc(v.scalar_tensor_tensor(out=hid[:, ts(hc, TPC)], in0=tanhb[:, 0:TPC],
                                          scalar=1.0, in1=x2p[:, 0:TPC],
                                          op0=ALU.add, op1=ALU.mult))

    @blk.tensor
    def _(t):
        for hc in range(NHC):
            t.wait_ge(dv.h, dv0 + 5 * hc + 1)
            if hc >= 1:
                t.wait_ge(sc.h, sc0 + 4 * hc)
                t.wait_ge(dv.h, dv0 + 5 * hc)
            for tt in range(2):
                for kc in range(NCH):
                    mm = t.matmul(ps[tt][:, :], qq[0][:, ts(kc, 128)],
                                  xn[:, kc * TPC + tt * 512:kc * TPC + tt * 512 + 512],
                                  start=(kc == 0), stop=(kc == NCH - 1))
                pe.inc(mm)
            t.wait_ge(dv.h, dv0 + 5 * hc + 2)
            for tt in range(2):
                for kc in range(NCH):
                    mm = t.matmul(ps[2 + tt][:, :], qq[1][:, ts(kc, 128)],
                                  xn[:, kc * TPC + tt * 512:kc * TPC + tt * 512 + 512],
                                  start=(kc == 0), stop=(kc == NCH - 1))
                pe.inc(mm)

    barrier(P, blk)
    P.af("cbw0", "cbw1", "tanhb", "x2p", "xn", "qq0", "qq1")

    if stage < 7:
        @blk.sync
        def _(s):
            s.wait_ge(dma.h, dma.n)
        @blk.gpsimd
        def _(g):
            g.wait_ge(dmg.h, dmg.n)
        P.end()
        return P
    HH = NHC * 128 // 2
    cb3 = [P.aa("cb3_0", NHC * 128, F32)] * 2
    q3 = [P.aa("q3_%d" % i, NHC * 128, BF16) for i in range(2)]
    ffnout = P.aa("ffnout", 2 * TPC, F32)

    dma0 = dma.n
    sc0 = sc.n
    dv0 = dv.n
    pe0 = pe.n

    @blk.sync
    def _(g):
        for dc in range(NCH):
            if dc >= 1:
                g.wait_ge(sc.h, sc0 + dc)
            g.dma_start(out=cb3[0].rearrange("p (c h) -> p c h", c=NHC),
                        in_=w3t_d[:, ts(dc, 128)].rearrange("(c p) h -> p c h", p=128)
                        ).then_inc(dma.h, 16)
            if dc >= 1:
                g.wait_ge(dv.h, dv0 + 3 * (dc - 1) + 3)
                g.dma_start(out=ffn_o[(dc - 1) * 128:dc * 128, :],
                            in_=ffnout[:, ((dc - 1) % 2) * TPC:((dc - 1) % 2) * TPC + TPC]
                            ).then_inc(dma.h, 16)
        g.wait_ge(dv.h, dv0 + 3 * (NCH - 1) + 3)
        g.dma_start(out=ffn_o[(NCH - 1) * 128:NCH * 128, :],
                    in_=ffnout[:, ((NCH - 1) % 2) * TPC:((NCH - 1) % 2) * TPC + TPC]
                    ).then_inc(dma.h, 16)
    dma.n += 32 * NCH

    @blk.scalar
    def _(s):
        for dc in range(NCH):
            pp = dc % 2
            s.wait_ge(dma.h, dma0 + 16 * (2 * dc if dc >= 1 else 1))
            if dc >= 1:
                s.wait_ge(dv.h, dv0 + 3 * dc - 2)
            s.activation(out=sgn[0][:, 0:HH], in_=cb3[pp][:, 0:HH],
                         func=AF.Sign, bias=thr[:, 5:6])
            s.activation(out=sgn[1][:, 0:HH], in_=cb3[pp][:, HH:2 * HH],
                         func=AF.Sign, bias=thr[:, 5:6])
            s.activation(out=sgn[2][:, 0:HH], in_=cb3[pp][:, 0:HH],
                         func=AF.Sign, bias=thr[:, 2:3])
            P.sc.inc(s.activation(out=sgn[3][:, 0:HH], in_=cb3[pp][:, HH:2 * HH],
                                  func=AF.Sign, bias=thr[:, 2:3]))

    @blk.vector
    def _(v):
        for dc in range(NCH):
            pp = dc % 2
            v.wait_ge(sc.h, sc0 + dc + 1)
            if dc >= 2:
                v.wait_ge(pe.h, pe0 + 2 * dc - 2)
            v.tensor_tensor(out=q3[pp][:, 0:HH], in0=sgn[0][:, 0:HH],
                            in1=sgn[2][:, 0:HH], op=ALU.add)
            dv.inc(v.tensor_tensor(out=q3[pp][:, HH:2 * HH], in0=sgn[1][:, 0:HH],
                                   in1=sgn[3][:, 0:HH], op=ALU.add))
            v.wait_ge(pe.h, pe0 + 2 * dc + 2)
            if dc >= 2:
                v.wait_ge(dma.h, dma0 + 16 * (2 * (dc - 2) + 3))
            for tt in range(2):
                dv.inc(v.tensor_scalar(out=ffnout[:, (dc % 2) * TPC + tt * 512:
                                                  (dc % 2) * TPC + tt * 512 + 512],
                                       in0=ps[4 + (dc % 2) * 2 + tt][:, :],
                                       scalar1=0.5, scalar2=None, op0=ALU.mult))

    @blk.tensor
    def _(t):
        for dc in range(NCH):
            pp = dc % 2
            t.wait_ge(dv.h, dv0 + 3 * dc + 1)
            if dc >= 2:
                t.wait_ge(dv.h, dv0 + 3 * dc - 3)
            for tt in range(2):
                for hcc in range(NHC):
                    mm = t.matmul(ps[4 + (dc % 2) * 2 + tt][:, :], q3[pp][:, ts(hcc, 128)],
                                  hid[:, hcc * TPC + tt * 512:hcc * TPC + tt * 512 + 512],
                                  start=(hcc == 0), stop=(hcc == NHC - 1))
                pe.inc(mm)

    if dbg:
        @blk.sync
        def _(s):
            s.dma_start(out=thr_o[:, :], in_=thr[:, 0:6]).then_inc(dma.h, 16)
            s.dma_start(out=hid_o[:, :], in_=hid[:, :]).then_inc(dma.h, 16)
            s.dma_start(out=cols_o[:, :], in_=cols[:, :]).then_inc(dma.h, 16)
            s.dma_start(out=trow_o[:, :], in_=trow[0:1, 0:12]).then_inc(dma.h, 16)
        dma.n += 64
    @blk.sync
    def _(s):
        s.wait_ge(dma.h, dma.n)
    @blk.gpsimd
    def _(g):
        g.wait_ge(dmg.h, dmg.n)

    P.end()
    return P


def _t(a):
    return np.ascontiguousarray(np.asarray(a).T)


def _packcol(a, nch):
    return np.ascontiguousarray(np.asarray(a).reshape(nch, 128).T)


def prep_l0_main(inp):
    x = np.asarray(inp["L0_attn_x"], dtype=np.float32)
    wdqkv = np.ascontiguousarray(
        np.concatenate([np.asarray(inp["L0_attn_W_dq"]), np.asarray(inp["L0_attn_W_dkv"])],
                       axis=1).astype(np.float32))
    lnp = np.ascontiguousarray(np.concatenate(
        [_packcol(inp["L0_attn_q_ln_g"], 4), _packcol(inp["L0_attn_q_ln_b"], 4),
         _packcol(inp["L0_attn_kv_ln_g"], 4), _packcol(inp["L0_attn_kv_ln_b"], 4)],
        axis=1).astype(np.float32))
    wukv = np.ascontiguousarray(np.asarray(inp["L0_attn_W_ukv"], dtype=np.float32))
    w1t = _t(np.asarray(inp["L0_ffn_w1"], dtype=np.float32))
    w2t = _t(np.asarray(inp["L0_ffn_w2"], dtype=np.float32))
    w3t = _t(np.asarray(inp["L0_ffn_w3"], dtype=np.float32))
    maps = []
    for c in range(8):
        b, g = c // 4, c % 4
        xs = _t(x[b, g * TPC:(g + 1) * TPC, :])
        maps.append(dict(x=xs, wdqkv=wdqkv, lnp=lnp, wukv=wukv,
                         w1t=w1t, w2t=w2t, w3t=w3t))
    return maps


_PROGS = {}


def _prog(name, builder):
    if name not in _PROGS:
        _PROGS[name] = builder()
    return _PROGS[name]


def kernel(**inputs):
    cores = list(range(8))
    P1 = _prog("l0_main", build_l0_main)
    r1 = run_bass_kernel_spmd(P1.nc, prep_l0_main(inputs), core_ids=cores).results
    P2 = _prog("l0_attn", build_l0_attn)
    r2 = run_bass_kernel_spmd(P2.nc, prep_l0_attn(inputs, r1), core_ids=cores).results
    P2b = _prog("l0_res", build_l0_res)
    r2b = run_bass_kernel_spmd(P2b.nc, prep_l0_res(inputs, r1, r2), core_ids=cores).results
    x1_list = [np.asarray(r["x1"], dtype=np.float32) for r in r2b]
    P3 = _prog("l1_local", build_l1_local)
    r3 = run_bass_kernel_spmd(P3.nc, prep_l1_local(inputs, x1_list), core_ids=cores).results
    P4 = _prog("l1_tail", build_l1_tail)
    r4 = run_bass_kernel_spmd(P4.nc, prep_l1_tail(inputs, r3, x1_list), core_ids=cores).results
    out = np.zeros((2, 4096, 2048), np.float32)
    for c in range(8):
        b, g = c // 4, c % 4
        out[b, g * TPC:(g + 1) * TPC, :] = np.asarray(r4[c]["fin"]).T
    return out




def build_l0_attn(dbg=False, nheads=4, nsb=8, wo=True):
    P = Prog()
    cqf_d = P.dram_in("cqf", [512, 4096], BF16)
    wuq_d = P.dram_in("wuq", [512, 512], F32)
    kf_d = P.dram_in("kf", [512, 4096], BF16)
    vf_d = P.dram_in("vf", [4096, 512], BF16)
    wot_d = P.dram_in("wot", [512, D], F32)
    part_o = P.dram_out("part", [D, 4096], F32)

    blk = P.begin()
    common_setup(P, blk)
    P.arena_init(46000)
    dma, dmg, pe, dv, sc, gp = P.dma, P.dmg, P.pe, P.dv, P.sc, P.gp
    ps = [P.psum("ps%d" % i, [128, 512], F32) for i in range(4)]
    psT = [P.psum("psT%d" % i, [128, 1024], BF16) for i in range(2)]
    psR = P.psum("psR", [128, 1024], BF16)

    cqf = P.aa("cqf", 4 * 4096, BF16)
    wuq = P.aa("wuq", 4 * 512, BF16)
    kh = P.aa("kh", 4096, BF16)
    vh = P.aa("vh", 4096, BF16)
    qh = P.aa("qh", 4096, BF16)
    srow = P.aa("srow", 4096, F32)
    pbuf = P.aa("pbuf", 4 * 4096, BF16)
    pts = [P.aa("pts%d" % i, 512, BF16) for i in range(2)]
    obuf = P.aa("obuf", 4 * 4096, BF16)
    rinvB = P.aa("rinvB", 512, F32)
    mx = P.sbuf("mx", [128, 16], F32)
    mcol = P.sbuf("mcol", [128, 1], F32)
    msc = P.sbuf("msc", [128, 1], F32)
    rs = P.sbuf("rs", [128, 1], F32)
    rinv = P.sbuf("rinv", [128, 1], F32)
    rhi = P.sbuf("rhi", [128, 1], BF16)
    rlo = P.sbuf("rlo", [128, 1], BF16)
    rrow = P.sbuf("rrowb", [1, 1024], BF16)
    nmask = P.sbuf("nmask", [128, 128], F32)

    @blk.vector
    def _(v):
        v.wait_ge(gp.h, gp.n)
        dv.inc(v.tensor_scalar(out=nmask[:, :], in0=P.iw[:, :], scalar1=0, scalar2=-1.0e30,
                               op0=ALU.is_lt, op1=ALU.mult))

    @blk.sync
    def _(s):
        s.dma_start(out=cqf.rearrange("p (c t) -> p c t", c=4),
                    in_=cqf_d.rearrange("(c p) t -> p c t", p=128)).then_inc(dma.h, 16)
    dma.n += 16

    @blk.gpsimd
    def _(g):
        g.dma_start(out=wuq.rearrange("p (c n) -> p c n", c=4),
                    in_=wuq_d.rearrange("(c p) n -> p c n", p=128)).then_inc(dmg.h, 16)
    dmg.n += 16
    lds = (dma.n, dmg.n, dv.n)

    pe_prev_head = 0
    for h in range(nheads):
        @blk.sync
        def _(s, h=h, pv=pe_prev_head):
            s.wait_ge(pe.h, pv)
            s.dma_start(out=kh[:, :], in_=kf_d[h * 128:h * 128 + 128, :]).then_inc(dma.h, 16)
            s.dma_start(out=vh.rearrange("p (c n) -> p c n", c=32),
                        in_=vf_d[:, h * 128:h * 128 + 128].rearrange("(c p) n -> p c n", p=128)
                        ).then_inc(dma.h, 16)
        dma.n += 32
        kv_ld = dma.n

        dv_qevac_base = None
        @blk.tensor
        def _(t, h=h, kv_ld=kv_ld, dvq=dv.n):
            t.wait_ge(dma.h, kv_ld)
            t.wait_ge(dmg.h, lds[1])
            t.wait_ge(dv.h, dvq)
            for tt in range(8):
                if tt >= 2:
                    t.wait_ge(dv.h, dvq + tt - 1)
                for kc in range(4):
                    mm = t.matmul(ps[2 + tt % 2][:, :],
                                  wuq[:, kc * 512 + h * 128:kc * 512 + h * 128 + 128],
                                  cqf[:, kc * 4096 + tt * 512:kc * 4096 + tt * 512 + 512],
                                  start=(kc == 0), stop=(kc == 3))
                pe.inc(mm)
        q_pe0 = pe.n - 8

        @blk.vector
        def _(v, q_pe0=q_pe0):
            for tt in range(8):
                v.wait_ge(pe.h, q_pe0 + tt + 1)
                dv.inc(v.tensor_copy(out=qh[:, ts(tt, 512)], in_=ps[2 + tt % 2][:, :]))
        q_dv = dv.n

        for sb in range(nsb):
            kmax_sb = (sb + 1) * 512
            nkc = kmax_sb // 128
            for qb in range(4):
                p0 = sb * 512 + qb * 128
                nfull = p0 // 512
                w = p0 + 128 - 512 * nfull
                dv_base = dv.n

                @blk.tensor
                def _(t, p0=p0, nfull=nfull, w=w, q_dv=q_dv, dv_base=dv_base, scn=sc.n):
                    t.wait_ge(dv.h, q_dv)
                    t.wait_ge(dv.h, dv_base)
                    t.wait_ge(sc.h, scn)
                    for kt in range(nfull):
                        if kt >= 2:
                            t.wait_ge(dv.h, dv_base + kt - 1)
                        pe.inc(t.matmul(ps[kt % 2][:, :], qh[:, p0:p0 + 128], kh[:, ts(kt, 512)],
                                        start=True, stop=True))
                    pe.inc(t.matmul(ps[2][:, 0:w], qh[:, p0:p0 + 128],
                                    kh[:, 512 * nfull:512 * nfull + w], start=True, stop=True))
                pe_base = pe.n - (nfull + 1)

                nmx = nfull + (1 if w > 128 else 0)
                @blk.vector
                def _(v, p0=p0, nfull=nfull, w=w, pe_base=pe_base, nmx=nmx):
                    for kt in range(nfull):
                        v.wait_ge(pe.h, pe_base + kt + 1)
                        dv.inc(v.tensor_scalar(out=srow[:, ts(kt, 512)], in0=ps[kt % 2][:, :],
                                               scalar1=1.0, scalar2=None, op0=ALU.mult,
                                               op1=ALU.max, accum_out=mx[:, kt:kt + 1]))
                    v.wait_ge(pe.h, pe_base + nfull + 1)
                    if w > 128:
                        dv.inc(v.tensor_scalar(out=srow[:, 512 * nfull:512 * nfull + w - 128],
                                               in0=ps[2][:, 0:w - 128], scalar1=1.0, scalar2=None,
                                               op0=ALU.mult, op1=ALU.max,
                                               accum_out=mx[:, nfull:nfull + 1]))
                    dv.inc(v.scalar_tensor_tensor(out=srow[:, p0:p0 + 128],
                                                  in0=ps[2][:, w - 128:w], scalar=1.0,
                                                  in1=nmask[:, :], op0=ALU.mult, op1=ALU.add))
                    v.wait_ge(dv.h, dv.n)
                    dv.inc(v.tensor_reduce(out=mx[:, nmx:nmx + 1], in_=srow[:, p0:p0 + 128],
                                           axis=AXX, op=ALU.max))
                    v.wait_ge(dv.h, dv.n)
                    dv.inc(v.tensor_reduce(out=mcol[:, 0:1], in_=mx[:, 0:nmx + 1],
                                           axis=AXX, op=ALU.max))
                m_dv = dv.n

                @blk.gpsimd
                def _(g, m_dv=m_dv, scn=sc.n, pen=pe.n, p0=p0, qb=qb, kmax_sb=kmax_sb):
                    g.wait_ge(sc.h, scn)
                    g.wait_ge(dv.h, m_dv)
                    gp.inc(g.tensor_scalar(out=msc[:, 0:1], in0=mcol[:, 0:1],
                                           scalar1=-1.0, scalar2=None, op0=ALU.mult))
                    if p0 + 128 < kmax_sb:
                        g.wait_ge(pe.h, pen)
                        gp.inc(g.memset(pbuf[:, qb * 4096 + p0 + 128:qb * 4096 + kmax_sb], 0.0))
                gp_msc = gp.n

                @blk.scalar
                def _(s, gp_msc=gp_msc, m_dv=m_dv, dvn=dv.n, p0=p0, qb=qb):
                    s.wait_ge(gp.h, gp_msc)
                    s.wait_ge(dv.h, dvn)
                    sc.inc(s.activation(out=pbuf[:, qb * 4096:qb * 4096 + p0 + 128],
                                        in_=srow[:, 0:p0 + 128], func=AF.Exp, scale=1.0,
                                        bias=msc[:, 0:1], accum_out=rs[:, 0:1]))
                exp_sc = sc.n

                @blk.vector
                def _(v, exp_sc=exp_sc, pen=pe.n):
                    v.wait_ge(sc.h, exp_sc)
                    v.wait_ge(pe.h, pen)
                    dv.inc(v.reciprocal(out=rinv[:, 0:1], in_=rs[:, 0:1]))
                    v.wait_ge(dv.h, dv.n)
                    dv.inc(v.tensor_scalar(out=rhi[:, 0:1], in0=rinv[:, 0:1], scalar1=1.0,
                                           scalar2=None, op0=ALU.mult))
                r_dv = dv.n

                @blk.gpsimd
                def _(g, r_dv=r_dv):
                    g.wait_ge(dv.h, r_dv)
                    gp.inc(g.tensor_tensor(out=rlo[:, 0:1], in0=rinv[:, 0:1], in1=rhi[:, 0:1],
                                           op=ALU.subtract))
                rlo_gp = gp.n

                @blk.tensor
                def _(t, r_dv=r_dv, rlo_gp=rlo_gp, qb=qb, dvn=dv.n):
                    t.wait_ge(dv.h, r_dv)
                    t.wait_ge(gp.h, rlo_gp)
                    t.transpose(psR[0:1, qb * 128:qb * 128 + 128], rhi[:, 0:1], P.ident_b[:, :])
                    pe.inc(t.transpose(psR[0:1, 512 + qb * 128:512 + qb * 128 + 128],
                                       rlo[:, 0:1], P.ident_b[:, :]))
            rowt_pe = pe.n

            @blk.vector
            def _(v, rowt_pe=rowt_pe):
                v.wait_ge(pe.h, rowt_pe)
                dv.inc(v.tensor_copy(out=rrow[0:1, :], in_=psR[0:1, :]))
            rrow_dv = dv.n

            @blk.tensor
            def _(t, rrow_dv=rrow_dv):
                t.wait_ge(dv.h, rrow_dv)
                t.matmul(ps[3][:, :], P.ones1_b[0:1, :], rrow[0:1, 0:512], start=True, stop=False)
                pe.inc(t.matmul(ps[3][:, :], P.ones1_b[0:1, :], rrow[0:1, 512:1024],
                                start=False, stop=True))
            rb_pe = pe.n

            @blk.vector
            def _(v, rb_pe=rb_pe):
                v.wait_ge(pe.h, rb_pe)
                dv.inc(v.tensor_copy(out=rinvB[:, :], in_=ps[3][:, :]))
            rinvB_dv = dv.n

            pv_dv_base = dv.n
            sc_all_exp = sc.n
            gp_tails = gp.n
            tr0 = pe.n
            @blk.tensor
            def _(t, nkc=nkc, pv_dv_base=pv_dv_base, sc_all_exp=sc_all_exp,
                  gp_tails=gp_tails):
                t.wait_ge(sc.h, sc_all_exp)
                t.wait_ge(gp.h, gp_tails)
                for kc in range(nkc):
                    if kc >= 2:
                        t.wait_ge(dv.h, pv_dv_base + kc - 1)
                    for qb in range(4):
                        tr = t.transpose(psT[kc % 2][:, qb * 128:qb * 128 + 128],
                                         pbuf[:, qb * 4096 + kc * 128:qb * 4096 + kc * 128 + 128],
                                         P.ident_b[:, :])
                    pe.inc(tr)
                    if kc >= 1:
                        t.wait_ge(dv.h, pv_dv_base + kc)
                        pe.inc(t.matmul(ps[2][:, :], vh[:, (kc - 1) * 128:kc * 128],
                                        pts[(kc - 1) % 2][:, :], start=(kc - 1 == 0),
                                        stop=False))
                t.wait_ge(dv.h, pv_dv_base + nkc)
                pe.inc(t.matmul(ps[2][:, :], vh[:, (nkc - 1) * 128:nkc * 128],
                                pts[(nkc - 1) % 2][:, :], start=(nkc == 1), stop=True))

            @blk.vector
            def _(v, nkc=nkc, tr0=tr0):
                for kc in range(nkc):
                    v.wait_ge(pe.h, tr0 + (1 if kc == 0 else 2 * kc))
                    dv.inc(v.tensor_copy(out=pts[kc % 2][:, :], in_=psT[kc % 2][:, 0:512]))
            pv_pe = pe.n

            @blk.vector
            def _(v, pv_pe=pv_pe, h=h, sb=sb, rinvB_dv=rinvB_dv):
                v.wait_ge(pe.h, pv_pe)
                v.wait_ge(dv.h, rinvB_dv)
                dv.inc(v.tensor_tensor(out=obuf[:, h * 4096 + sb * 512:h * 4096 + sb * 512 + 512],
                                       in0=ps[2][:, :], in1=rinvB[:, :], op=ALU.mult))
        pe_prev_head = pe.n

    if not wo:
        obuf_dv = dv.n
        @blk.sync
        def _(s):
            s.wait_ge(dv.h, dv.n)
            s.dma_start(out=part_o[0:128, :], in_=obuf.bitcast(F32)[:, 0:4096]).then_inc(dma.h, 16)
            dma.n += 16
            s.wait_ge(dma.h, dma.n)
        @blk.gpsimd
        def _(g):
            g.wait_ge(dmg.h, dmg.n)
        P.end()
        return P
    if not dbg:
        P.af("cqf", "kh", "vh", "qh", "srow", "pbuf", "pts0", "pts1")
    wob = [P.aa("wob%d" % i, 4 * 128, BF16) for i in range(2)]
    pout = P.aa("pout", 2 * 4096, F32)
    obuf_dv = dv.n
    dmg0 = dmg.n
    dma_st0 = dma.n
    sc0w = sc.n

    @blk.gpsimd
    def _(g, obuf_dv=obuf_dv):
        g.wait_ge(dv.h, obuf_dv)
        for dc in range(NCH):
            if dc >= 2:
                g.wait_ge(pe.h, pe_prev_head + 8 * (dc - 1))
            g.dma_start(out=wob[dc % 2].rearrange("p (c n) -> p c n", c=4),
                        in_=wot_d[:, ts(dc, 128)].rearrange("(c p) n -> p c n", p=128)
                        ).then_inc(dmg.h, 16)
    dmg.n += 16 * NCH

    wo_dv0 = dv.n
    @blk.tensor
    def _(t):
        for dc in range(NCH):
            t.wait_ge(dmg.h, dmg0 + 16 * dc + 16)
            for tt in range(8):
                if dc * 8 + tt >= 2:
                    t.wait_ge(dv.h, wo_dv0 + dc * 8 + tt - 1)
                for oc in range(4):
                    mm = t.matmul(ps[tt % 2][:, :], wob[dc % 2][:, oc * 128:oc * 128 + 128],
                                  obuf[:, oc * 4096 + tt * 512:oc * 4096 + tt * 512 + 512],
                                  start=(oc == 0), stop=(oc == 3))
                pe.inc(mm)
    wo_pe0 = pe.n - 8 * NCH

    @blk.vector
    def _(v):
        for dc in range(NCH):
            for tt in range(8):
                v.wait_ge(pe.h, wo_pe0 + 8 * dc + tt + 1)
                if dc >= 2 and tt == 0:
                    v.wait_ge(dma.h, dma_st0 + 16 * (dc - 1))
                dv.inc(v.tensor_copy(out=pout[:, (dc % 2) * 4096 + tt * 512:
                                              (dc % 2) * 4096 + tt * 512 + 512],
                                     in_=ps[tt % 2][:, :]))
    @blk.sync
    def _(s):
        for dc in range(NCH):
            s.wait_ge(dv.h, wo_dv0 + 8 * dc + 8)
            s.dma_start(out=part_o[ts(dc, 128), :],
                        in_=pout[:, (dc % 2) * 4096:(dc % 2) * 4096 + 4096]).then_inc(dma.h, 16)
        s.wait_ge(dma.h, dma.n + 16 * NCH)
    dma.n += 16 * NCH

    @blk.gpsimd
    def _(g):
        g.wait_ge(dmg.h, dmg.n)

    P.end()
    return P


def prep_l0_attn(inp, r1):
    x = np.asarray(inp["L0_attn_x"], dtype=np.float32)
    wuq_full = np.asarray(inp["L0_attn_W_uq"], dtype=np.float32)
    wot_full = np.ascontiguousarray(np.asarray(inp["L0_attn_W_o"], dtype=np.float32).T)
    maps = []
    for c in range(8):
        b, g = c // 4, c % 4
        h0 = 4 * g
        cqf = np.concatenate([r1[4 * b + gg]["cq"] for gg in range(4)], axis=1)
        kf = np.concatenate([r1[4 * b + gg]["kown"] for gg in range(4)], axis=1)
        vf = np.concatenate([r1[4 * b + gg]["vown"] for gg in range(4)], axis=0)
        kf = kf[h0 * 128:h0 * 128 + 512, :]
        vf = vf[:, h0 * 128:h0 * 128 + 512]
        wuq = np.ascontiguousarray(wuq_full[:, h0 * 128:h0 * 128 + 512])
        wot = np.ascontiguousarray(wot_full[h0 * 128:h0 * 128 + 512, :])
        maps.append(dict(cqf=np.ascontiguousarray(cqf), kf=np.ascontiguousarray(kf),
                         vf=np.ascontiguousarray(vf), wuq=wuq, wot=wot))
    return maps



def build_l0_res():
    P = Prog()
    x_d = P.dram_in("x", [D, TPC], F32)
    f_d = P.dram_in("ffn0", [D, TPC], F32)
    p_d = [P.dram_in("p%d" % i, [D, TPC], F32) for i in range(4)]
    x1_o = P.dram_out("x1", [D, TPC], F32)

    blk = P.begin()
    common_setup(P, blk)
    P.arena_init(46000)
    dma, dv = P.dma, P.dv
    a = P.aa("a", NCH * TPC, F32)
    b_ = P.aa("b", NCH * TPC, F32)

    @blk.sync
    def _(s):
        s.dma_start(out=a.rearrange("p (c t) -> p c t", c=NCH),
                    in_=x_d.rearrange("(c p) t -> p c t", p=128)).then_inc(dma.h, 16)
        s.dma_start(out=b_.rearrange("p (c t) -> p c t", c=NCH),
                    in_=f_d.rearrange("(c p) t -> p c t", p=128)).then_inc(dma.h, 16)
    dma.n += 32

    @blk.vector
    def _(v):
        v.wait_ge(dma.h, dma.n)
        dv.inc(v.tensor_tensor(out=a[:, :], in0=a[:, :], in1=b_[:, :], op=ALU.add))

    for i in range(4):
        @blk.sync
        def _(s, i=i, dvn=dv.n):
            s.wait_ge(dv.h, dvn)
            s.dma_start(out=b_.rearrange("p (c t) -> p c t", c=NCH),
                        in_=p_d[i].rearrange("(c p) t -> p c t", p=128)).then_inc(dma.h, 16)
        dma.n += 16
        ld = dma.n

        @blk.vector
        def _(v, i=i, ld=ld):
            v.wait_ge(dma.h, ld)
            dv.inc(v.tensor_tensor(out=a[:, :], in0=a[:, :], in1=b_[:, :], op=ALU.add))

    @blk.sync
    def _(s):
        s.wait_ge(dv.h, dv.n)
        s.dma_start(out=x1_o.rearrange("(c p) t -> p c t", p=128),
                    in_=a.rearrange("p (c t) -> p c t", c=NCH)).then_inc(dma.h, 16)
        dma.n += 16
        s.wait_ge(dma.h, dma.n)

    P.end()
    return P


def prep_l0_res(inp, r1, r2):
    x = np.asarray(inp["L0_attn_x"], dtype=np.float32)
    maps = []
    for c in range(8):
        b, g = c // 4, c % 4
        sl = slice(g * TPC, (g + 1) * TPC)
        maps.append(dict(
            x=_t(x[b, sl, :]),
            ffn0=r1[c]["ffn0"],
            **{"p%d" % i: np.ascontiguousarray(r2[4 * b + i]["part"][:, sl]) for i in range(4)},
        ))
    return maps




def _tiles(T):
    out = []
    o = 0
    while o < T:
        w = min(512, T - o)
        out.append((o, w))
        o += w
    return out


def emit_ln2(P, blk, ps, raw, nch, T, invn, out_cb):
    tl = _tiles(T)
    nt = len(tl)
    assert 2 * nt <= 6
    sq = P.lnsq
    srow = P.lnsrow
    brow = P.lnbrow
    vrow = P.lnvrow
    rrow = P.lnrrow
    mnB = P.lnmnB
    rstdB = P.lnrstdB
    tmp = P.lntmp

    @blk.tensor
    def _(t):
        for c in range(nch):
            for i, (o, w) in enumerate(tl):
                mm = t.matmul(ps[i][0:1, 0:w], P.ones128_b[:, 0:1], raw(c)[:, o:o + w],
                              start=(c == 0), stop=(c == nch - 1))
        P.pe.inc(mm)

    sc0 = P.sc.n
    pe_sq0 = P.pe.n

    @blk.scalar
    def _(s):
        k = 0
        for c in range(nch):
            for i, (o, w) in enumerate(tl):
                if k >= 2:
                    s.wait_ge(P.pe.h, pe_sq0 + k - 1)
                P.sc.inc(s.activation(out=sq[:, (k % 2) * 512:(k % 2) * 512 + w],
                                      in_=raw(c)[:, o:o + w], func=AF.Square))
                k += 1

    @blk.tensor
    def _(t):
        k = 0
        for c in range(nch):
            for i, (o, w) in enumerate(tl):
                t.wait_ge(P.sc.h, sc0 + k + 1)
                P.pe.inc(t.matmul(ps[nt + i][0:1, 0:w], P.ones128_b[:, 0:1],
                                  sq[:, (k % 2) * 512:(k % 2) * 512 + w],
                                  start=(c == 0), stop=(c == nch - 1)))
                k += 1
    sums_pe = P.pe.n

    @blk.vector
    def _(v):
        v.wait_ge(P.pe.h, sums_pe)
        for i, (o, w) in enumerate(tl):
            v.tensor_scalar(out=srow[0:1, o:o + w], in0=ps[i][0:1, 0:w],
                            scalar1=invn, scalar2=None, op0=ALU.mult)
            v.tensor_scalar(out=srow[0:1, T + o:T + o + w], in0=ps[nt + i][0:1, 0:w],
                            scalar1=invn, scalar2=None, op0=ALU.mult)
        v.tensor_tensor(out=vrow[0:1, 0:T], in0=srow[0:1, 0:T], in1=srow[0:1, 0:T], op=ALU.mult)
        P.dv.inc(v.tensor_tensor(out=vrow[0:1, 0:T], in0=srow[0:1, T:2 * T],
                                 in1=vrow[0:1, 0:T], op=ALU.subtract))
    stats_dv = P.dv.n

    @blk.scalar
    def _(s):
        s.wait_ge(P.dv.h, stats_dv)
        s.activation(out=rrow[0:1, 0:T], in_=vrow[0:1, 0:T], func=AF.Ln, bias=P.eps5[0:1, 0:1])
        P.sc.inc(s.activation(out=rrow[0:1, 0:T], in_=rrow[0:1, 0:T], func=AF.Exp, scale=-0.5))
    rstd_sc = P.sc.n

    @blk.vector
    def _(v):
        v.wait_ge(P.sc.h, rstd_sc)
        v.tensor_copy(out=brow[0:1, 0:T], in_=srow[0:1, 0:T])
        v.tensor_copy(out=brow[0:1, T:2 * T], in_=rrow[0:1, 0:T])
        v.tensor_tensor(out=brow[0:1, 2 * T:3 * T], in0=srow[0:1, 0:T],
                        in1=brow[0:1, 0:T], op=ALU.subtract)
        P.dv.inc(v.tensor_tensor(out=brow[0:1, 3 * T:4 * T], in0=rrow[0:1, 0:T],
                                 in1=brow[0:1, T:2 * T], op=ALU.subtract))
    rows_dv = P.dv.n

    @blk.tensor
    def _(t):
        t.wait_ge(P.dv.h, rows_dv)
        for i, (o, w) in enumerate(tl):
            t.matmul(ps[i][:, 0:w], P.ones1_b[0:1, :], brow[0:1, o:o + w],
                     start=True, stop=False)
            t.matmul(ps[i][:, 0:w], P.ones1_b[0:1, :], brow[0:1, 2 * T + o:2 * T + o + w],
                     start=False, stop=True)
            t.matmul(ps[nt + i][:, 0:w], P.ones1_b[0:1, :], brow[0:1, T + o:T + o + w],
                     start=True, stop=False)
            mm = t.matmul(ps[nt + i][:, 0:w], P.ones1_b[0:1, :],
                          brow[0:1, 3 * T + o:3 * T + o + w], start=False, stop=True)
        P.pe.inc(mm)
    bc_pe = P.pe.n

    @blk.vector
    def _(v):
        v.wait_ge(P.pe.h, bc_pe)
        for i, (o, w) in enumerate(tl):
            v.tensor_copy(out=mnB[:, o:o + w], in_=ps[i][0:128, 0:w])
            v.tensor_copy(out=rstdB[:, o:o + w], in_=ps[nt + i][0:128, 0:w])
        for c in range(nch):
            v.tensor_tensor(out=tmp[:, 0:T], in0=raw(c)[:, 0:T], in1=mnB[:, 0:T], op=ALU.subtract)
            last = out_cb(c, v, tmp, rstdB)
            if c == nch - 1:
                P.dv.inc(last)
    return P.dv.n



TA = 1028
OB = 4
NXC = 34

def build_l1_local():
    P = Prog()
    x1a_d = P.dram_in("x1a", [D, TA], F32)
    win_d = P.dram_in("win", [D, 8512], F32)
    mupz_d = P.dram_in("mupz", [1, 4096], F32)
    mupx_d = P.dram_in("mupx", [128, NXC], F32)
    mupdt_d = P.dram_in("mupdt", [64, 1], F32)
    cw_d = P.dram_in("cw", [128, NXC * 4], F32)
    cb_d = P.dram_in("cb", [128, NXC], F32)
    dtb_d = P.dram_in("dtb", [64, 1], F32)
    alog_d = P.dram_in("alog", [64, 1], F32)
    dvec_d = P.dram_in("dvec", [1, 64], F32)

    xn1_o = P.dram_out("xn1", [D, TPC], BF16)
    z_o = P.dram_out("z", [TPC, 4096], BF16)
    y_o = P.dram_out("ylocal", [TPC, 4096], BF16)
    c_o = P.dram_out("cfm", [128, TPC], BF16)
    spub_o = P.dram_out("spub", [64, 128, 64], F32)
    lam_o = P.dram_out("lam", [64, 1], F32)
    expa_o = P.dram_out("expa", [TPC, 64], BF16)

    blk = P.begin()
    common_setup(P, blk)
    P.arena_init(51500)
    dma, dmg, pe, dv, sc, gp = P.dma, P.dmg, P.pe, P.dv, P.sc, P.gp
    ps = [P.psum("ps%d" % i, [128, 512], F32) for i in range(6)]
    psb = [P.psum("psb%d" % i, [128, 1024], BF16) for i in range(2)]

    ones64 = P.sbuf("ones64", [64, 128], F32)
    masku = P.sbuf("masku", [128, 128], F32)
    mupdt = P.sbuf("mupdt_s", [64, 1], F32)
    dtb = P.sbuf("dtb_s", [64, 1], F32)
    alog = P.sbuf("alog_s", [64, 1], F32)
    negea = P.sbuf("negea", [64, 1], F32)
    dvec = P.sbuf("dvec_s", [1, 64], F32)
    mupx = P.sbuf("mupx_s", [128, NXC], F32)
    cw = P.sbuf("cw_s", [128, NXC * 4], F32)
    cbb = P.sbuf("cb_s", [128, NXC], F32)
    DB = P.sbuf("DB", [128, 64], F32)
    lamc = P.sbuf("lamc", [64, 1], F32)

    x1a = P.aa("x1a", 16 * TA, BF16)
    xn1 = P.aa("xn1", 16 * TA, BF16)

    @blk.gpsimd
    def _(g):
        g.memset(ones64[:, :], 1.0)
        g.dma_start(out=x1a.rearrange("p (c t) -> p c t", c=16),
                    in_=x1a_d.rearrange("(c p) t -> p c t", p=128)).then_inc(dmg.h, 16)
    dmg.n += 16
    x_ld = dmg.n

    @blk.vector
    def _(v):
        v.wait_ge(gp.h, gp.n)
        dv.inc(v.tensor_scalar(out=masku[:, :], in0=P.iw[:, :], scalar1=0, scalar2=-1.0e30,
                               op0=ALU.is_gt, op1=ALU.mult))

    @blk.sync
    def _(s):
        s.dma_start(out=mupdt[:, :], in_=mupdt_d[:, :]).then_inc(dma.h, 16)
        s.dma_start(out=dtb[:, :], in_=dtb_d[:, :]).then_inc(dma.h, 16)
        s.dma_start(out=alog[:, :], in_=alog_d[:, :]).then_inc(dma.h, 16)
        s.dma_start(out=dvec[:, :], in_=dvec_d[:, :]).then_inc(dma.h, 16)
        s.dma_start(out=mupx[:, :], in_=mupx_d[:, :]).then_inc(dma.h, 16)
        s.dma_start(out=cw[:, :], in_=cw_d[:, :]).then_inc(dma.h, 16)
        s.dma_start(out=cbb[:, :], in_=cb_d[:, :]).then_inc(dma.h, 16)
    dma.n += 7 * 16
    small_ld = dma.n

    @blk.scalar
    def _(s):
        s.wait_ge(dma.h, small_ld)
        sc.inc(s.activation(out=negea[:, :], in_=alog[:, :], func=AF.Exp))
    @blk.vector
    def _(v, scn=sc.n):
        v.wait_ge(sc.h, scn)
        dv.inc(v.tensor_scalar(out=negea[:, :], in0=negea[:, :], scalar1=-1.0, scalar2=None,
                               op0=ALU.mult))

    @blk.tensor
    def _(t, dvn=dv.n):
        t.wait_ge(dma.h, small_ld)
        t.wait_ge(dv.h, dvn)
        pe.inc(t.matmul(ps[5][:, 0:64], P.ones1_f[0:1, :], dvec[0:1, :], start=True, stop=True))
    @blk.vector
    def _(v, pen=pe.n):
        v.wait_ge(pe.h, pen)
        dv.inc(v.tensor_copy(out=DB[:, :], in_=ps[5][:, 0:64]))

    alloc_ln_bufs(P, T=TA)

    @blk.tensor
    def _(t, xl=x_ld):
        t.wait_ge(dmg.h, xl)
        t.wait_ge(dv.h, dv.n)
        pe.inc(t.matmul(ps[5][0:1, 0:1], P.ones128_b[:, 0:1], x1a[:, 0:1], start=True, stop=True))
    @blk.scalar
    def _(s, xl=x_ld):
        s.wait_ge(dmg.h, xl)
        sc.inc(s.activation(out=P.eps5[:, 0:1], in_=P.eps5[:, 0:1], func=AF.Identity))

    def xn1_out(c, v, tmp, rstdB):
        return v.tensor_tensor(out=xn1[:, ts(c, TA)], in0=tmp[:, 0:TA], in1=rstdB[:, 0:TA],
                               op=ALU.mult)

    emit_ln2(P, blk, ps, lambda c: x1a[:, ts(c, TA)], 16, TA, 1.0 / D, xn1_out)
    barrier(P, blk)
    P.af("x1a")

    @blk.sync
    def _(s):
        s.dma_start(out=xn1_o.rearrange("(c p) t -> p c t", p=128),
                    in_=xn1.rearrange("p (c t) -> p c t", c=16)[:, :, OB:OB + TPC]
                    ).then_inc(dma.h, 16)
    dma.n += 16
    free_ln_bufs(P)

    xbc = P.aa("xbc", NXC * TA, BF16)
    wx = [P.aa("wx%d" % i, 16 * 256, BF16) for i in range(2)]
    dmg0 = dmg.n
    pe_x0 = pe.n
    dv_x0 = dv.n
    tlx = _tiles(TA)

    @blk.gpsimd
    def _(g):
        for gx in range(17):
            if gx >= 2:
                g.wait_ge(pe.h, pe_x0 + 6 * (gx - 1))
            g.dma_start(out=wx[gx % 2].rearrange("p (c n) -> p c n", c=16),
                        in_=win_d[:, 4096 + gx * 256:4096 + gx * 256 + 256]
                        .rearrange("(c p) n -> p c n", p=128)).then_inc(dmg.h, 16)
    dmg.n += 17 * 16

    @blk.tensor
    def _(t):
        xi = 0
        for gx in range(17):
            t.wait_ge(dmg.h, dmg0 + 16 * gx + 16)
            for cc in range(2):
                for (o, w) in tlx:
                    if xi >= 6:
                        t.wait_ge(dv.h, dv_x0 + xi - 5)
                    for kc in range(16):
                        mm = t.matmul(ps[xi % 6][:, 0:w],
                                      wx[gx % 2][:, kc * 256 + cc * 128:kc * 256 + cc * 128 + 128],
                                      xn1[:, kc * TA + o:kc * TA + o + w],
                                      start=(kc == 0), stop=(kc == 15))
                    pe.inc(mm)
                    xi += 1

    @blk.vector
    def _(v):
        xi = 0
        for gx in range(17):
            for cc in range(2):
                c = gx * 2 + cc
                for (o, w) in tlx:
                    v.wait_ge(pe.h, pe_x0 + xi + 1)
                    dv.inc(v.tensor_scalar(out=xbc[:, c * TA + o:c * TA + o + w],
                                           in0=ps[xi % 6][:, 0:w], scalar1=mupx[:, c:c + 1],
                                           scalar2=None, op0=ALU.mult))
                    xi += 1
    barrier(P, blk)
    P.af("wx0", "wx1")

    dtraw = P.aa("dtraw", TA, F32)
    wd = P.aa("wd", 16 * 64, BF16)
    dmg0 = dmg.n
    pe_d0 = pe.n

    @blk.gpsimd
    def _(g):
        g.dma_start(out=wd.rearrange("p (c n) -> p c n", c=16),
                    in_=win_d[:, 8448:8512].rearrange("(c p) n -> p c n", p=128)
                    ).then_inc(dmg.h, 16)
    dmg.n += 16

    @blk.tensor
    def _(t):
        t.wait_ge(dmg.h, dmg.n)
        for i, (o, w) in enumerate(tlx):
            for kc in range(16):
                mm = t.matmul(ps[i][0:64, 0:w], wd[:, kc * 64:kc * 64 + 64],
                              xn1[:, kc * TA + o:kc * TA + o + w],
                              start=(kc == 0), stop=(kc == 15))
            pe.inc(mm)
    pe_dt = pe.n

    @blk.vector
    def _(v):
        for i, (o, w) in enumerate(tlx):
            v.wait_ge(pe.h, pe_d0 + i + 1)
            dv.inc(v.tensor_scalar(out=dtraw[0:64, o:o + w], in0=ps[i][0:64, 0:w],
                                   scalar1=mupdt[:, 0:1], scalar2=None, op0=ALU.mult))
    barrier(P, blk)

    xbcc = P.aa("xbcc", NXC * TPC, BF16)
    acc = [P.aa("acc%d" % i, TPC, F32) for i in range(2)]
    dv_c0 = dv.n
    sc_c0 = sc.n

    @blk.vector
    def _(v):
        for c in range(NXC):
            if c >= 1:
                v.wait_ge(sc.h, sc_c0 + c)
            v.tensor_scalar(out=acc[0][:, :], in0=xbc[:, c * TA + 1:c * TA + 1 + TPC],
                            scalar1=cw[:, c * 4:c * 4 + 1], scalar2=cbb[:, c:c + 1],
                            op0=ALU.mult, op1=ALU.add)
            v.scalar_tensor_tensor(out=acc[1][:, :], in0=xbc[:, c * TA + 2:c * TA + 2 + TPC],
                                   scalar=cw[:, c * 4 + 1:c * 4 + 2], in1=acc[0][:, :],
                                   op0=ALU.mult, op1=ALU.add)
            v.scalar_tensor_tensor(out=acc[0][:, :], in0=xbc[:, c * TA + 3:c * TA + 3 + TPC],
                                   scalar=cw[:, c * 4 + 2:c * 4 + 3], in1=acc[1][:, :],
                                   op0=ALU.mult, op1=ALU.add)
            dv.inc(v.scalar_tensor_tensor(out=acc[1][:, :],
                                          in0=xbc[:, c * TA + 4:c * TA + 4 + TPC],
                                          scalar=cw[:, c * 4 + 3:c * 4 + 4], in1=acc[0][:, :],
                                          op0=ALU.mult, op1=ALU.add))

    @blk.scalar
    def _(s):
        for c in range(NXC):
            s.wait_ge(dv.h, dv_c0 + c + 1)
            sc.inc(s.activation(out=xbcc[:, ts(c, TPC)], in_=acc[1][:, :], func=AF.Silu))
    barrier(P, blk)
    P.af("xbc", "acc0", "acc1")

    @blk.sync
    def _(s):
        s.dma_start(out=c_o[:, :], in_=xbcc[:, 33 * TPC:34 * TPC]).then_inc(dma.h, 16)
    dma.n += 16

    mupzr = P.aa("mupzr", 4096, F32)
    mupzh = P.aa("mupzh", 4096, BF16)
    mupzl = P.aa("mupzl", 4096, BF16)
    mupzB = P.aa("mupzB", 4096, F32)

    @blk.sync
    def _(s, dvn=dv.n):
        s.wait_ge(dv.h, dvn)
        s.dma_start(out=mupzr[0:1, :], in_=mupz_d[:, :]).then_inc(dma.h, 16)
    dma.n += 16
    mz_ld = dma.n

    @blk.vector
    def _(v):
        v.wait_ge(dma.h, mz_ld)
        v.tensor_copy(out=mupzh[0:1, :], in_=mupzr[0:1, :])
        v.wait_ge(dv.h, dv.n)
        dv.inc(v.tensor_tensor(out=mupzl[0:1, :], in0=mupzr[0:1, :], in1=mupzh[0:1, :],
                               op=ALU.subtract))
    dv_m0 = dv.n
    @blk.tensor
    def _(t, dvn=dv.n):
        t.wait_ge(dv.h, dvn)
        for zt in range(8):
            if zt >= 2:
                t.wait_ge(dv.h, dv_m0 + zt - 1)
            t.matmul(ps[zt % 2][:, :], P.ones1_b[0:1, :], mupzh[0:1, ts(zt, 512)],
                     start=True, stop=False)
            pe.inc(t.matmul(ps[zt % 2][:, :], P.ones1_b[0:1, :], mupzl[0:1, ts(zt, 512)],
                            start=False, stop=True))
    mzb_pe0 = pe.n - 8

    @blk.vector
    def _(v):
        for zt in range(8):
            v.wait_ge(pe.h, mzb_pe0 + zt + 1)
            dv.inc(v.tensor_copy(out=mupzB[:, ts(zt, 512)], in_=ps[zt % 2][:, :]))
    barrier(P, blk)
    P.af("mupzr", "mupzh", "mupzl")

    wz = [P.aa("wz%d" % i, 16 * 512, BF16) for i in range(2)]
    zstage = P.aa("zstage", 2 * 512, BF16)
    dmg0 = dmg.n
    pe_z0 = pe.n
    dv_z0 = dv.n
    dma_z0 = dma.n

    @blk.gpsimd
    def _(g):
        for cb in range(8):
            if cb >= 2:
                g.wait_ge(pe.h, pe_z0 + 8 * (cb - 1))
            g.dma_start(out=wz[cb % 2].rearrange("p (c n) -> p c n", c=16),
                        in_=win_d[:, cb * 512:cb * 512 + 512].rearrange("(c p) n -> p c n", p=128)
                        ).then_inc(dmg.h, 16)
    dmg.n += 8 * 16

    @blk.tensor
    def _(t):
        for cb in range(8):
            t.wait_ge(dmg.h, dmg0 + 16 * cb + 16)
            for tc in range(8):
                zi = cb * 8 + tc
                if zi >= 6:
                    t.wait_ge(dv.h, dv_z0 + zi - 5)
                for kc in range(16):
                    mm = t.matmul(ps[zi % 6][:, :],
                                  xn1[:, kc * TA + OB + tc * 128:kc * TA + OB + tc * 128 + 128],
                                  wz[cb % 2][:, kc * 512:kc * 512 + 512],
                                  start=(kc == 0), stop=(kc == 15))
                pe.inc(mm)

    @blk.vector
    def _(v):
        for cb in range(8):
            for tc in range(8):
                zi = cb * 8 + tc
                v.wait_ge(pe.h, pe_z0 + zi + 1)
                if zi >= 2:
                    v.wait_ge(dma.h, dma_z0 + 16 * (zi - 1))
                dv.inc(v.tensor_tensor(out=zstage[:, (zi % 2) * 512:(zi % 2) * 512 + 512],
                                       in0=ps[zi % 6][:, :], in1=mupzB[:, cb * 512:cb * 512 + 512],
                                       op=ALU.mult))

    @blk.sync
    def _(s):
        for cb in range(8):
            for tc in range(8):
                zi = cb * 8 + tc
                s.wait_ge(dv.h, dv_z0 + zi + 1)
                s.dma_start(out=z_o[tc * 128:tc * 128 + 128, cb * 512:cb * 512 + 512],
                            in_=zstage[:, (zi % 2) * 512:(zi % 2) * 512 + 512]
                            ).then_inc(dma.h, 16)
    dma.n += 64 * 16
    barrier(P, blk)
    P.af("wz0", "wz1", "zstage", "mupzB")

    P.af("xn1")
    dtf = P.aa("dtf", TPC, F32)
    lvec = P.aa("lvec", TPC, F32)
    tmp64 = P.aa("tmp64", TPC, F32)
    acum = P.aa("acum", TPC, F32)
    acumT = P.aa("acumT", 512, F32)
    nacumT = P.aa("nacumT", 512, F32)
    expAT = P.aa("expAT", 512, F32)
    dtT = P.aa("dtT", 512, F32)
    expRun = P.aa("expRun", 512, BF16)
    lamB = P.aa("lamB", 512, F32)
    al8 = P.sbuf("al8", [64, 8], F32)
    lamall = P.sbuf("lamall", [64, 8], F32)
    ppi = P.sbuf("ppi", [64, 8], F32)
    zeros64 = P.sbuf("zeros64", [64, 8], F32)
    rowA = P.sbuf("rowA", [1, 64], F32)
    rowB = P.sbuf("rowB", [1, 64], F32)

    @blk.gpsimd
    def _(g):
        gp.inc(g.memset(zeros64[:, :], 0.0))

    @blk.vector
    def _(v):
        dv.inc(v.tensor_scalar(out=dtf[0:64, :], in0=dtraw[0:64, OB:OB + TPC],
                               scalar1=dtb[:, 0:1], scalar2=None, op0=ALU.add))
        v.wait_ge(dv.h, dv.n)
        dv.inc(v.tensor_scalar(out=lvec[0:64, :], in0=dtf[0:64, :], scalar1=0.0, scalar2=None,
                               op0=ALU.max))
    dtc_dv = dv.n

    @blk.scalar
    def _(s):
        s.wait_ge(dv.h, dtc_dv)
        sc.inc(s.activation(out=tmp64[0:64, :], in_=dtf[0:64, :], func=AF.Abs))
        s.wait_ge(sc.h, sc.n)
        sc.inc(s.activation(out=tmp64[0:64, :], in_=tmp64[0:64, :], func=AF.Exp, scale=-1.0))
        s.wait_ge(sc.h, sc.n)
        sc.inc(s.activation(out=tmp64[0:64, :], in_=tmp64[0:64, :], func=AF.Ln, bias=1.0))
    sp_sc = sc.n

    @blk.vector
    def _(v):
        v.wait_ge(sc.h, sp_sc)
        dv.inc(v.tensor_tensor(out=dtf[0:64, :], in0=lvec[0:64, :], in1=tmp64[0:64, :],
                               op=ALU.add))
        v.wait_ge(dv.h, dv.n)
        dv.inc(v.tensor_scalar(out=lvec[0:64, :], in0=dtf[0:64, :], scalar1=negea[:, 0:1],
                               scalar2=None, op0=ALU.mult))
        v.wait_ge(dv.h, dv.n)
        for c in range(8):
            sc_ = v.tensor_tensor_scan(out=acum[0:64, ts(c, 128)], data0=ones64[:, :],
                                       data1=lvec[0:64, ts(c, 128)], initial=0.0,
                                       op0=ALU.mult, op1=ALU.add)
        dv.inc(sc_)
        v.wait_ge(dv.h, dv.n)
        for c in range(8):
            cp = v.tensor_copy(out=al8[:, c:c + 1], in_=acum[0:64, c * 128 + 127:c * 128 + 128])
        dv.inc(cp)
    acum_dv = dv.n

    @blk.scalar
    def _(s):
        s.wait_ge(dv.h, acum_dv)
        sc.inc(s.activation(out=lamall[:, :], in_=al8[:, :], func=AF.Exp))
    lam_sc = sc.n

    @blk.vector
    def _(v, gpn=gp.n):
        v.wait_ge(sc.h, lam_sc)
        v.wait_ge(gp.h, gpn)
        dv.inc(v.tensor_tensor_scan(out=ppi[:, :], data0=lamall[:, :], data1=zeros64[:, :],
                                    initial=1.0, op0=ALU.mult, op1=ALU.add))
    ppi_dv = dv.n

    @blk.tensor
    def _(t, acum_dv=acum_dv, evbase=dv.n):
        t.wait_ge(dv.h, acum_dv)
        for c in range(8):
            if c >= 2:
                t.wait_ge(dv.h, evbase + c - 1)
            t.transpose(ps[c % 2][:, 0:64], acum[0:64, ts(c, 128)], P.ident_f[0:64, 0:64])
            pe.inc(t.transpose(ps[c % 2][:, 64:128], dtf[0:64, ts(c, 128)], P.ident_f[0:64, 0:64]))
    tr_pe0 = pe.n - 8

    @blk.vector
    def _(v):
        for c in range(8):
            v.wait_ge(pe.h, tr_pe0 + c + 1)
            v.tensor_copy(out=acumT[:, ts(c, 64)], in_=ps[c % 2][:, 0:64])
            dv.inc(v.tensor_copy(out=dtT[:, ts(c, 64)], in_=ps[c % 2][:, 64:128]))
        v.wait_ge(dv.h, dv.n)
        dv.inc(v.tensor_scalar(out=nacumT[:, :], in0=acumT[:, :], scalar1=-1.0, scalar2=None,
                               op0=ALU.mult))
    nac_dv = dv.n

    @blk.scalar
    def _(s):
        s.wait_ge(dv.h, nac_dv)
        sc.inc(s.activation(out=expAT[:, :], in_=acumT[:, :], func=AF.Exp))
    expAT_sc = sc.n

    for c in range(8):
        @blk.tensor
        def _(t, c=c, ppi_dv=ppi_dv, lam_sc=lam_sc, dvn=dv.n):
            t.wait_ge(dv.h, max(ppi_dv, dvn))
            t.wait_ge(sc.h, lam_sc)
            t.transpose(ps[2][0:1, 0:64], lamall[:, c:c + 1], P.ident_f[0:64, 0:64])
            if c >= 1:
                pe.inc(t.transpose(ps[2][0:1, 64:128], ppi[:, c - 1:c], P.ident_f[0:64, 0:64]))
            else:
                pe.inc(t.transpose(ps[2][0:1, 64:128], zeros64[:, 0:1], P.ident_f[0:64, 0:64]))
        rt_pe = pe.n

        @blk.vector
        def _(v, rt_pe=rt_pe):
            v.wait_ge(pe.h, rt_pe)
            v.tensor_copy(out=rowA[0:1, :], in_=ps[2][0:1, 0:64])
            dv.inc(v.tensor_copy(out=rowB[0:1, :], in_=ps[2][0:1, 64:128]))
        rows_dv = dv.n

        @blk.tensor
        def _(t, rows_dv=rows_dv):
            t.wait_ge(dv.h, rows_dv)
            t.matmul(ps[3][:, 0:64], P.ones1_f[0:1, :], rowA[0:1, :], start=True, stop=True)
            pe.inc(t.matmul(ps[3][:, 64:128], P.ones1_f[0:1, :], rowB[0:1, :],
                            start=True, stop=True))
        bc_pe = P.pe.n

        @blk.vector
        def _(v, c=c, bc_pe=bc_pe, expAT_sc=expAT_sc):
            v.wait_ge(pe.h, bc_pe)
            v.wait_ge(sc.h, expAT_sc)
            dv.inc(v.tensor_copy(out=lamB[:, ts(c, 64)], in_=ps[3][0:128, 0:64]))
            if c >= 1:
                dv.inc(v.tensor_tensor(out=expRun[:, ts(c, 64)], in0=expAT[:, ts(c, 64)],
                                       in1=ps[3][0:128, 64:128], op=ALU.mult))
            else:
                dv.inc(v.tensor_copy(out=expRun[:, ts(c, 64)], in_=expAT[:, ts(c, 64)]))
    barrier(P, blk)

    @blk.sync
    def _(s):
        s.dma_start(out=expa_o.rearrange("(c p) h -> p c h", p=128),
                    in_=expRun.rearrange("p (c h) -> p c h", c=8)).then_inc(dma.h, 16)
    dma.n += 16

    @blk.vector
    def _(v):
        dv.inc(v.tensor_copy(out=lamc[:, 0:1], in_=ppi[:, 7:8]))
    lamr_dv = dv.n

    @blk.sync
    def _(s):
        s.wait_ge(dv.h, lamr_dv)
        s.dma_start(out=lam_o[:, :], in_=lamc[:, 0:1]).then_inc(dma.h, 16)
    dma.n += 16

    P.af("dtraw", "dtf", "lvec", "tmp64", "acum")
    ybuf = P.aa("ybuf", 8 * 4096, BF16)
    S_sb = P.aa("S_sb", 64 * 64, F32)
    S_bf = P.aa("S_bf", 64 * 64, BF16)
    G_sb = P.aa("G_sb", 128, BF16)
    Btm = P.aa("Btm", 128, BF16)
    Ein = P.aa("Ein", 128, F32)
    Ebf = P.aa("Ebf", 128, BF16)
    Pmat = P.aa("Pmat", 128, BF16)
    xtm = P.aa("xtm", 64, BF16)
    xwtm = P.aa("xwtm", 64, BF16)
    dtw = P.sbuf("dtw", [128, 1], F32)
    frow = P.sbuf("frow", [1, 128], F32)
    yp = P.aa("yp", 64, F32)

    @blk.gpsimd
    def _(g):
        gp.inc(g.memset(S_sb[:, :], 0.0))
        gp.inc(g.memset(S_bf[:, :], 0.0))

    barrier(P, blk)

    for c in range(8):
        @blk.tensor
        def _(t, c=c, dvn=dv.n):
            t.wait_ge(dv.h, dvn)
            pe.inc(t.matmul(ps[5][:, 0:128], xbcc[:, 32 * TPC + c * 128:32 * TPC + c * 128 + 128],
                            xbcc[:, 33 * TPC + c * 128:33 * TPC + c * 128 + 128],
                            start=True, stop=True))
            pe.inc(t.transpose(psb[1][:, 0:128],
                               xbcc[:, 32 * TPC + c * 128:32 * TPC + c * 128 + 128],
                               P.ident_b[:, :]))
        gb_pe = pe.n

        @blk.vector
        def _(v, gb_pe=gb_pe):
            v.wait_ge(pe.h, gb_pe)
            dv.inc(v.tensor_copy(out=G_sb[:, 0:128], in_=ps[5][:, 0:128]))
            dv.inc(v.tensor_copy(out=Btm[:, 0:128], in_=psb[1][:, 0:128]))
        gb_dv = dv.n

        for h in range(64):
            ch = c * 64 + h
            hb = h % 2
            @blk.tensor
            def _(t, c=c, h=h, ch=ch, dvn=dv.n, gb_dv=gb_dv):
                t.wait_ge(dv.h, max(dvn, gb_dv))
                pe.inc(t.transpose(ps[4][0:1, 384:512], acumT[:, ch:ch + 1], P.ident_f[:, :]))
            fr_pe = pe.n

            @blk.vector
            def _(v, fr_pe=fr_pe):
                v.wait_ge(pe.h, fr_pe)
                dv.inc(v.tensor_copy(out=frow[0:1, :], in_=ps[4][0:1, 384:512]))
            fr_dv = dv.n

            @blk.tensor
            def _(t, fr_dv=fr_dv):
                t.wait_ge(dv.h, fr_dv)
                pe.inc(t.matmul(ps[4][:, 0:128], P.ones1_f[0:1, :], frow[0:1, :],
                                start=True, stop=True))
            R_pe = pe.n

            @blk.vector
            def _(v, ch=ch, R_pe=R_pe):
                v.wait_ge(pe.h, R_pe)
                dv.inc(v.scalar_tensor_tensor(out=Ein[:, 0:128], in0=ps[4][:, 0:128],
                                              scalar=nacumT[:, ch:ch + 1], in1=masku[:, :],
                                              op0=ALU.add, op1=ALU.add))
            ein_dv = dv.n

            @blk.scalar
            def _(s, ein_dv=ein_dv):
                s.wait_ge(dv.h, ein_dv)
                sc.inc(s.activation(out=Ebf[:, 0:128], in_=Ein[:, 0:128], func=AF.Exp))
            e_sc = sc.n

            @blk.gpsimd
            def _(g, ch=ch, e_sc=e_sc):
                g.wait_ge(sc.h, e_sc)
                gp.inc(g.tensor_tensor(out=dtw[:, 0:1], in0=dtT[:, ch:ch + 1],
                                       in1=Ebf[:, 127:128], op=ALU.mult))
            dtw_gp = gp.n

            @blk.vector
            def _(v, ch=ch, e_sc=e_sc, gb_dv=gb_dv):
                v.wait_ge(sc.h, e_sc)
                dv.inc(v.scalar_tensor_tensor(out=Pmat[:, 0:128], in0=Ebf[:, 0:128],
                                              scalar=dtT[:, ch:ch + 1], in1=G_sb[:, 0:128],
                                              op0=ALU.mult, op1=ALU.mult))
            pm_dv = dv.n

            @blk.tensor
            def _(t, c=c, h=h, dvn=dv.n):
                t.wait_ge(dv.h, dvn)
                hb2 = 64 * (h % 2)
                pe.inc(t.transpose(psb[0][:, 0:64],
                                   xbcc[hb2:hb2 + 64,
                                        (h // 2) * TPC + c * 128:(h // 2) * TPC + c * 128 + 128],
                                   P.ident_b[hb2:hb2 + 64, hb2:hb2 + 64]))
            xt_pe = pe.n

            @blk.vector
            def _(v, xt_pe=xt_pe, dtw_gp=dtw_gp):
                v.wait_ge(pe.h, xt_pe)
                v.wait_ge(gp.h, dtw_gp)
                dv.inc(v.tensor_copy(out=xtm[:, 0:64], in_=psb[0][:, 0:64]))
                dv.inc(v.tensor_scalar(out=xwtm[:, 0:64], in0=psb[0][:, 0:64],
                                       scalar1=dtw[:, 0:1], scalar2=None, op0=ALU.mult))
            xt_dv = dv.n

            @blk.tensor
            def _(t, c=c, h=h, hb=hb, pm_dv=pm_dv, xt_dv=xt_dv, dvn=dv.n):
                t.wait_ge(dv.h, max(xt_dv, dvn))
                t.wait_ge(dv.h, pm_dv)
                pe.inc(t.matmul(ps[2][:, 0:64],
                                xbcc[:, 33 * TPC + c * 128:33 * TPC + c * 128 + 128],
                                S_bf[:, h * 64:h * 64 + 64], start=True, stop=True))
                pe.inc(t.matmul(ps[hb][:, 0:64], Pmat[:, 0:128], xtm[:, 0:64],
                                start=True, stop=True))
                pe.inc(t.matmul(ps[3][:, 0:64], Btm[:, 0:128], xwtm[:, 0:64],
                                start=True, stop=True))
            mm_pe = pe.n

            @blk.vector
            def _(v, c=c, h=h, hb=hb, ch=ch, mm_pe=mm_pe):
                v.wait_ge(pe.h, mm_pe)
                dv.inc(v.scalar_tensor_tensor(out=yp[:, 0:64], in0=xtm[:, 0:64],
                                              scalar=DB[:, h:h + 1], in1=ps[hb][:, 0:64],
                                              op0=ALU.mult, op1=ALU.add))
                v.wait_ge(dv.h, dv.n)
                dv.inc(v.scalar_tensor_tensor(
                    out=ybuf[:, c * 4096 + h * 64:c * 4096 + h * 64 + 64],
                    in0=ps[2][:, 0:64], scalar=expAT[:, ch:ch + 1],
                    in1=yp[:, 0:64], op0=ALU.mult, op1=ALU.add))
                dv.inc(v.scalar_tensor_tensor(out=S_sb[:, h * 64:h * 64 + 64],
                                              in0=S_sb[:, h * 64:h * 64 + 64],
                                              scalar=lamB[:, ch:ch + 1], in1=ps[3][:, 0:64],
                                              op0=ALU.mult, op1=ALU.add))
                v.wait_ge(dv.h, dv.n)
                dv.inc(v.tensor_copy(out=S_bf[:, h * 64:h * 64 + 64],
                                     in_=S_sb[:, h * 64:h * 64 + 64]))
    barrier(P, blk)

    @blk.sync
    def _(s):
        s.dma_start(out=y_o.rearrange("(c p) n -> p c n", p=128),
                    in_=ybuf.rearrange("p (c n) -> p c n", c=8)).then_inc(dma.h, 16)
        s.dma_start(out=spub_o.rearrange("h p q -> p h q"), in_=S_sb.rearrange("p (h q) -> p h q", h=64)
                    ).then_inc(dma.h, 16)
        dma.n += 32
        s.wait_ge(dma.h, dma.n)
    @blk.gpsimd
    def _(g):
        g.wait_ge(dmg.h, dmg.n)

    P.end()
    return P


def prep_l1_local(inp, x1_list):
    win = np.ascontiguousarray(np.asarray(inp["L1_attn_W_in"], dtype=np.float32))
    mup = np.asarray(inp["L1_attn_mup_vector"], dtype=np.float32)
    cw = np.asarray(inp["L1_attn_conv_w"], dtype=np.float32)
    cb = np.asarray(inp["L1_attn_conv_b"], dtype=np.float32)
    maps = []
    mupz = np.ascontiguousarray(mup[0:4096][None, :])
    mupx = np.ascontiguousarray(mup[4096:8448].reshape(NXC, 128).T)
    mupdt = np.ascontiguousarray(mup[8448:8512][:, None])
    cwp = np.ascontiguousarray(np.transpose(cw.reshape(NXC, 128, 4), (1, 0, 2)).reshape(128, NXC * 4))
    cbp = np.ascontiguousarray(cb.reshape(NXC, 128).T)
    dtb = np.ascontiguousarray(np.asarray(inp["L1_attn_dt_bias"], dtype=np.float32)[:, None])
    alog = np.ascontiguousarray(np.asarray(inp["L1_attn_A_log"], dtype=np.float32)[:, None])
    dvec = np.ascontiguousarray(np.asarray(inp["L1_attn_D"], dtype=np.float32)[None, :])
    for c in range(8):
        b, g = c // 4, c % 4
        x1a = np.zeros((D, TA), np.float32)
        x1a[:, OB:OB + TPC] = x1_list[c]
        if g > 0:
            x1a[:, 1:OB] = x1_list[c - 1][:, TPC - 3:TPC]
        maps.append(dict(x1a=x1a, win=win, mupz=mupz, mupx=mupx, mupdt=mupdt,
                         cw=cwp, cb=cbp, dtb=dtb, alog=alog, dvec=dvec))
    return maps



def build_l1_tail():
    P = Prog()
    spri_d = P.dram_in("spri", [3, 64, 128 * 64], F32)
    lpri_d = P.dram_in("lpri", [64, 2], F32)
    cfm_d = P.dram_in("cfm", [128, TPC], BF16)
    expa_d = P.dram_in("expa", [TPC, 64], BF16)
    yl_d = P.dram_in("ylocal", [TPC, 4096], BF16)
    z_d = P.dram_in("z", [TPC, 4096], BF16)
    xn1_d = P.dram_in("xn1", [D, TPC], BF16)
    x1_d = P.dram_in("x1", [D, TPC], F32)
    nw_d = P.dram_in("nw", [128, 32], F32)
    wout_d = P.dram_in("wout", [4096, D], F32)
    wr_d = P.dram_in("wr", [D, 8], F32)
    br_d = P.dram_in("br", [1, 8], F32)
    wd_d = P.dram_in("wdall", [D, 128], F32)
    bd_d = P.dram_in("bd", [1, 128], F32)
    wu_d = P.dram_in("wuall", [128, D], F32)
    bu_d = P.dram_in("bu", [8, D], F32)
    fin_o = P.dram_out("fin", [D, TPC], F32)
    entry_t = P.dram_tmp("entry_t", [64, 128 * 64], F32)

    blk = P.begin()
    common_setup(P, blk)
    P.arena_init(48000)
    dma, dmg, pe, dv, sc, gp = P.dma, P.dmg, P.pe, P.dv, P.sc, P.gp
    ps = [P.psum("ps%d" % i, [128, 512], F32) for i in range(6)]
    psb = [P.psum("psb%d" % i, [128, 1024], BF16) for i in range(2)]

    nw = P.sbuf("nw_s", [128, 32], F32)
    brB = P.sbuf("brB", [128, 8], F32)
    bdB = P.sbuf("bdB", [128, 128], F32)
    bu_bf = P.sbuf("bu_bf", [8, D], BF16)
    wT_buf = P.sbuf("wT_buf", [8, TPC], BF16)
    m1 = P.sbuf("m1c", [128, 1], F32)
    negm1 = P.sbuf("negm1", [128, 1], F32)
    m2 = P.sbuf("m2c", [128, 1], F32)
    den = P.sbuf("den", [128, 1], F32)
    rinv = P.sbuf("rinvc", [128, 1], F32)
    ssq = P.sbuf("ssq", [128, 1], F32)
    ms = P.sbuf("ms", [128, 1], F32)
    rstd = P.sbuf("rstd", [128, 1], F32)
    lg = P.sbuf("lg", [128, 8], F32)
    eq = P.sbuf("eq", [128, 8], F32)
    X2 = P.sbuf("X2", [128, 8], F32)
    wexp = P.sbuf("wexp", [128, 8], F32)
    mask2 = P.sbuf("mask2", [128, 8], F32)
    w0 = P.sbuf("w0", [128, 8], F32)
    wbf = P.sbuf("wbf", [128, 8], BF16)
    lp = P.sbuf("lp", [64, 3], F32)

    cfm = P.aa("cfm", TPC, BF16)
    expaT = P.aa("expaT", 8 * 64, F32)
    xn1 = P.aa("xn1", 16 * TPC, BF16)
    wd_bf = P.aa("wd_bf", 16 * 128, BF16)
    wu_bf = P.aa("wu_bf", D, BF16)
    wr_bf = P.aa("wr_bf", 16 * 8, BF16)

    @blk.sync
    def _(s):
        s.dma_start(out=nw[:, :], in_=nw_d[:, :]).then_inc(dma.h, 16)
        s.dma_start(out=lp[:, 0:2], in_=lpri_d[:, :]).then_inc(dma.h, 16)
        s.dma_start(out=cfm[:, :], in_=cfm_d[:, :]).then_inc(dma.h, 16)
    dma.n += 48

    @blk.gpsimd
    def _(g):
        g.dma_start(out=xn1.rearrange("p (c t) -> p c t", c=16),
                    in_=xn1_d.rearrange("(c p) t -> p c t", p=128)).then_inc(dmg.h, 16)
        g.dma_start(out=expaT.rearrange("p (c h) -> p c h", c=8),
                    in_=expa_d.rearrange("(c p) h -> p c h", p=128)).then_inc(dmg.h, 16)
        g.dma_start(out=wd_bf.rearrange("p (c n) -> p c n", c=16),
                    in_=wd_d.rearrange("(c p) n -> p c n", p=128)).then_inc(dmg.h, 16)
        g.dma_start(out=wu_bf[:, :], in_=wu_d[:, :]).then_inc(dmg.h, 16)
        g.dma_start(out=wr_bf.rearrange("p (c n) -> p c n", c=16),
                    in_=wr_d.rearrange("(c p) n -> p c n", p=128)).then_inc(dmg.h, 16)
        g.dma_start(out=bu_bf[:, :], in_=bu_d[:, :]).then_inc(dmg.h, 16)
    dmg.n += 96
    lds = (dma.n, dmg.n)

    brow = P.sbuf("brow_s", [1, 8], F32)
    bdrow = P.sbuf("bdrow_s", [1, 128], F32)

    @blk.sync
    def _(s):
        s.dma_start(out=brow[:, :], in_=br_d[:, :]).then_inc(dma.h, 16)
        s.dma_start(out=bdrow[:, :], in_=bd_d[:, :]).then_inc(dma.h, 16)
    dma.n += 32
    br_ld = dma.n

    @blk.tensor
    def _(t, br_ld=br_ld):
        t.wait_ge(dma.h, br_ld)
        t.wait_ge(dv.h, P.consts_dv)
        t.matmul(ps[4][:, 0:8], P.ones1_f[0:1, :], brow[0:1, :], start=True, stop=True)
        pe.inc(t.matmul(ps[4][:, 8:136], P.ones1_f[0:1, :], bdrow[0:1, :], start=True, stop=True))
    bc_pe = pe.n

    @blk.vector
    def _(v, bc_pe=bc_pe):
        v.wait_ge(pe.h, bc_pe)
        v.tensor_copy(out=brB[:, :], in_=ps[4][:, 0:8])
        dv.inc(v.tensor_copy(out=bdB[:, :], in_=ps[4][:, 8:136]))
    bcast_dv = dv.n

    sprA = P.aa("sprA", 8192, F32)
    sprB = P.aa("sprB", 8192, F32)
    entry_bf = P.aa("entry_bf", 64 * 64, BF16)

    @blk.sync
    def _(s):
        s.dma_start(out=sprA[0:64, :], in_=spri_d[0, :, :]).then_inc(dma.h, 16)
        s.dma_start(out=sprB[0:64, :], in_=spri_d[1, :, :]).then_inc(dma.h, 16)
    dma.n += 32
    sp_ld = dma.n

    @blk.gpsimd
    def _(g, sp_ld=sp_ld):
        g.wait_ge(dma.h, sp_ld)
        gp.inc(g.tensor_tensor(out=lp[:, 2:3], in0=lp[:, 0:1], in1=lp[:, 1:2], op=ALU.mult))
    c2_gp = gp.n

    @blk.vector
    def _(v, sp_ld=sp_ld):
        v.wait_ge(dma.h, sp_ld)
        dv.inc(v.scalar_tensor_tensor(out=sprA[0:64, :], in0=sprB[0:64, :], scalar=lp[:, 0:1],
                                      in1=sprA[0:64, :], op0=ALU.mult, op1=ALU.add))
    e1_dv = dv.n

    @blk.sync
    def _(s):
        s.dma_start(out=sprB[0:64, :], in_=spri_d[2, :, :]).then_inc(dma.h, 16)
    dma.n += 16
    sp2_ld = dma.n

    @blk.vector
    def _(v, sp2_ld=sp2_ld, c2_gp=c2_gp, e1_dv=e1_dv):
        v.wait_ge(dma.h, sp2_ld)
        v.wait_ge(gp.h, c2_gp)
        v.wait_ge(dv.h, e1_dv)
        dv.inc(v.scalar_tensor_tensor(out=sprA[0:64, :], in0=sprB[0:64, :], scalar=lp[:, 2:3],
                                      in1=sprA[0:64, :], op0=ALU.mult, op1=ALU.add))
    e2_dv = dv.n

    @blk.sync
    def _(s, e2_dv=e2_dv):
        s.wait_ge(dv.h, e2_dv)
        s.dma_start(out=entry_t.ap()[:, :], in_=sprA[0:64, :]).then_inc(dma.h, 16)
    dma.n += 16
    et_st = dma.n

    @blk.gpsimd
    def _(g, et_st=et_st):
        g.wait_ge(dma.h, et_st)
        g.dma_start(out=entry_bf.rearrange("p (h q) -> p h q", h=64),
                    in_=entry_t.ap().rearrange("h (s q) -> s h q", s=128)).then_inc(dmg.h, 16)
    dmg.n += 16
    eb_ld = dmg.n
    P.af("sprA", "sprB")

    gT = P.aa("gT", 32 * TPC, BF16)
    whT_buf = P.aa("whT", 8 * 128, BF16)
    yrow = P.aa("yrow", 4096, BF16)
    zrow = P.aa("zrow", 4096, BF16)
    grow = P.aa("grow", 4096, BF16)
    sqd = P.aa("sqd", 4096, BF16)
    hrelu = P.aa("hrelu", 128, BF16)
    htmp = P.aa("htmp", 128, F32)
    wh = P.aa("wh", 128, BF16)
    ylq = P.aa("ylq", 4096, BF16)
    barrier(P, blk)

    for c in range(8):
        @blk.sync
        def _(s, c=c, dvn=dv.n):
            s.wait_ge(dv.h, dvn)
            s.dma_start(out=ylq[:, :], in_=yl_d[c * 128:c * 128 + 128, :]).then_inc(dma.h, 16)
            s.dma_start(out=zrow[:, :], in_=z_d[c * 128:c * 128 + 128, :]).then_inc(dma.h, 16)
        dma.n += 32
        row_ld = dma.n

        @blk.tensor
        def _(t, c=c, row_ld=row_ld, eb=eb_ld, dvn=dv.n):
            t.wait_ge(dmg.h, eb)
            t.wait_ge(dma.h, row_ld)
            t.wait_ge(dv.h, dvn)
            for h in range(64):
                if h >= 2:
                    t.wait_ge(dv.h, dvn + h - 1)
                pe.inc(t.matmul(ps[h % 2][:, 0:64], cfm[:, c * 128:c * 128 + 128],
                                entry_bf[:, h * 64:h * 64 + 64], start=True, stop=True))
        yi_pe0 = pe.n - 64

        @blk.vector
        def _(v, c=c, yi_pe0=yi_pe0):
            for h in range(64):
                v.wait_ge(pe.h, yi_pe0 + h + 1)
                dv.inc(v.scalar_tensor_tensor(out=yrow[:, h * 64:h * 64 + 64],
                                              in0=ps[h % 2][:, 0:64],
                                              scalar=expaT[:, c * 64 + h:c * 64 + h + 1],
                                              in1=ylq[:, h * 64:h * 64 + 64],
                                              op0=ALU.mult, op1=ALU.add))
        y_dv = dv.n

        @blk.scalar
        def _(s, y_dv=y_dv, row_ld=row_ld):
            s.wait_ge(dma.h, row_ld)
            s.wait_ge(dv.h, y_dv)
            sc.inc(s.activation(out=zrow[:, :], in_=zrow[:, :], func=AF.Silu))
        sil_sc = sc.n

        @blk.vector
        def _(v, sil_sc=sil_sc):
            v.wait_ge(sc.h, sil_sc)
            dv.inc(v.tensor_tensor(out=grow[:, :], in0=yrow[:, :], in1=zrow[:, :], op=ALU.mult))
        g_dv = dv.n

        @blk.scalar
        def _(s, g_dv=g_dv):
            s.wait_ge(dv.h, g_dv)
            sc.inc(s.activation(out=sqd[:, :], in_=grow[:, :], func=AF.Square,
                                accum_out=ssq[:, 0:1]))
        sq_sc = sc.n

        @blk.gpsimd
        def _(g, sq_sc=sq_sc):
            g.wait_ge(sc.h, sq_sc)
            gp.inc(g.tensor_scalar(out=ms[:, 0:1], in0=ssq[:, 0:1], scalar1=1.0 / 4096,
                                   scalar2=None, op0=ALU.mult))
        ms_gp = gp.n

        @blk.scalar
        def _(s, ms_gp=ms_gp):
            s.wait_ge(gp.h, ms_gp)
            s.activation(out=rstd[:, 0:1], in_=ms[:, 0:1], func=AF.Ln, bias=P.eps5[:, 0:1])
            sc.inc(s.activation(out=rstd[:, 0:1], in_=rstd[:, 0:1], func=AF.Exp, scale=-0.5))
        rstd_sc = sc.n

        @blk.vector
        def _(v, rstd_sc=rstd_sc):
            v.wait_ge(sc.h, rstd_sc)
            dv.inc(v.tensor_scalar(out=grow[:, :], in0=grow[:, :], scalar1=rstd[:, 0:1],
                                   scalar2=None, op0=ALU.mult))
        gs_dv = dv.n

        @blk.tensor
        def _(t, c=c, gs_dv=gs_dv, dvbase=dv.n):
            t.wait_ge(dv.h, gs_dv)
            for zc in range(32):
                if zc >= 2:
                    t.wait_ge(dv.h, dvbase + zc - 1)
                pe.inc(t.transpose(psb[zc % 2][:, 0:128], grow[:, zc * 128:zc * 128 + 128],
                                   P.ident_b[:, :]))
        gt_pe0 = pe.n - 32

        @blk.vector
        def _(v, c=c, gt_pe0=gt_pe0):
            for zc in range(32):
                v.wait_ge(pe.h, gt_pe0 + zc + 1)
                dv.inc(v.tensor_scalar(out=gT[:, zc * TPC + c * 128:zc * TPC + c * 128 + 128],
                                       in0=psb[zc % 2][:, 0:128], scalar1=nw[:, zc:zc + 1],
                                       scalar2=None, op0=ALU.mult))
        gt_dv = dv.n

        @blk.tensor
        def _(t, c=c, gt_dv=gt_dv, lds=lds):
            t.wait_ge(dmg.h, lds[1])
            t.wait_ge(dv.h, gt_dv)
            for kc in range(16):
                mm = t.matmul(ps[2][:, 0:8], xn1[:, kc * TPC + c * 128:kc * TPC + c * 128 + 128],
                              wr_bf[:, kc * 8:kc * 8 + 8], start=(kc == 0), stop=(kc == 15))
            pe.inc(mm)
        lg_pe = pe.n

        @blk.vector
        def _(v, lg_pe=lg_pe, bcast_dv=bcast_dv):
            v.wait_ge(pe.h, lg_pe)
            v.wait_ge(dv.h, bcast_dv)
            dv.inc(v.tensor_tensor(out=lg[:, :], in0=ps[2][:, 0:8], in1=brB[:, :], op=ALU.add))
            v.wait_ge(dv.h, dv.n)
            dv.inc(v.tensor_reduce(out=m1[:, 0:1], in_=lg[:, :], axis=AXX, op=ALU.max))
            v.wait_ge(dv.h, dv.n)
            dv.inc(v.tensor_scalar(out=negm1[:, 0:1], in0=m1[:, 0:1], scalar1=-1.0,
                                   scalar2=None, op0=ALU.mult))
            v.wait_ge(dv.h, dv.n)
            dv.inc(v.tensor_scalar(out=eq[:, :], in0=lg[:, :], scalar1=m1[:, 0:1],
                                   scalar2=None, op0=ALU.is_ge))
            v.wait_ge(dv.h, dv.n)
            dv.inc(v.scalar_tensor_tensor(out=X2[:, :], in0=eq[:, :], scalar=-1.0e30,
                                          in1=lg[:, :], op0=ALU.mult, op1=ALU.add))
            v.wait_ge(dv.h, dv.n)
            dv.inc(v.tensor_reduce(out=m2[:, 0:1], in_=X2[:, :], axis=AXX, op=ALU.max))
            v.wait_ge(dv.h, dv.n)
            dv.inc(v.tensor_scalar(out=mask2[:, :], in0=lg[:, :], scalar1=m2[:, 0:1],
                                   scalar2=None, op0=ALU.is_ge))
        moe_dv = dv.n

        @blk.scalar
        def _(s, moe_dv=moe_dv):
            s.wait_ge(dv.h, moe_dv)
            sc.inc(s.activation(out=wexp[:, :], in_=lg[:, :], func=AF.Exp, bias=negm1[:, 0:1]))
        we_sc = sc.n

        @blk.vector
        def _(v, we_sc=we_sc):
            v.wait_ge(sc.h, we_sc)
            dv.inc(v.tensor_tensor(out=w0[:, :], in0=wexp[:, :], in1=mask2[:, :], op=ALU.mult))
            v.wait_ge(dv.h, dv.n)
            dv.inc(v.tensor_scalar(out=X2[:, :], in0=w0[:, :], scalar1=1.0, scalar2=None,
                                   op0=ALU.mult, op1=ALU.add, accum_out=den[:, 0:1]))
            v.wait_ge(dv.h, dv.n)
            dv.inc(v.reciprocal(out=rinv[:, 0:1], in_=den[:, 0:1]))
            v.wait_ge(dv.h, dv.n)
            dv.inc(v.tensor_scalar(out=wbf[:, :], in0=w0[:, :], scalar1=rinv[:, 0:1],
                                   scalar2=None, op0=ALU.mult))
        w_dv = dv.n

        @blk.tensor
        def _(t, c=c, w_dv=w_dv):
            t.wait_ge(dv.h, w_dv)
            for kc in range(16):
                mm = t.matmul(ps[3][:, 0:128], xn1[:, kc * TPC + c * 128:kc * TPC + c * 128 + 128],
                              wd_bf[:, kc * 128:kc * 128 + 128], start=(kc == 0), stop=(kc == 15))
            pe.inc(mm)
        h_pe = pe.n

        @blk.vector
        def _(v, h_pe=h_pe):
            v.wait_ge(pe.h, h_pe)
            dv.inc(v.tensor_tensor(out=htmp[:, 0:128], in0=ps[3][:, 0:128], in1=bdB[:, :],
                                   op=ALU.add))
            v.wait_ge(dv.h, dv.n)
            dv.inc(v.tensor_scalar(out=hrelu[:, 0:128], in0=htmp[:, 0:128], scalar1=0.0,
                                   scalar2=None, op0=ALU.max))
            v.wait_ge(dv.h, dv.n)
            dv.inc(v.tensor_tensor(
                out=wh.rearrange("p (e r) -> p e r", e=8),
                in0=hrelu.rearrange("p (e r) -> p e r", e=8),
                in1=wbf.rearrange("p (e u) -> p e u", u=1).broadcast_to((128, 8, 16)),
                op=ALU.mult))
        wh_dv = dv.n

        @blk.tensor
        def _(t, c=c, wh_dv=wh_dv, dvn=dv.n):
            t.wait_ge(dv.h, wh_dv)
            t.transpose(psb[0][:, 128:256], wh[:, 0:128], P.ident_b[:, :])
            pe.inc(t.transpose(psb[0][0:8, 256:384], wbf[:, 0:8], P.ident_b[:, :]))
        wt_pe = pe.n

        @blk.vector
        def _(v, c=c, wt_pe=wt_pe):
            v.wait_ge(pe.h, wt_pe)
            dv.inc(v.tensor_copy(out=whT_buf[:, c * 128:c * 128 + 128], in_=psb[0][:, 128:256]))
            dv.inc(v.tensor_copy(out=wT_buf[0:8, c * 128:c * 128 + 128], in_=psb[0][0:8, 256:384]))
    barrier(P, blk)
    P.af("ylq", "yrow", "zrow", "grow", "sqd", "entry_bf")

    wob = [P.aa("wob%d" % i, 32 * 128, BF16) for i in range(2)]
    x1c = [P.aa("x1c%d" % i, TPC, F32) for i in range(2)]
    fout = P.aa("fout", 2 * TPC, F32)
    dmg0 = dmg.n
    dma_f0 = dma.n
    fin_dv0 = dv.n

    @blk.gpsimd
    def _(g, peb=pe.n):
        for dc in range(NCH):
            if dc >= 2:
                g.wait_ge(pe.h, peb + 2 * (dc - 1))
            g.dma_start(out=wob[dc % 2].rearrange("p (c n) -> p c n", c=32),
                        in_=wout_d[:, ts(dc, 128)].rearrange("(c p) n -> p c n", p=128)
                        ).then_inc(dmg.h, 16)
    dmg.n += 16 * NCH

    @blk.sync
    def _(s):
        for dc in range(NCH):
            if dc >= 2:
                s.wait_ge(dv.h, fin_dv0 + 2 * (dc - 1))
            s.dma_start(out=x1c[dc % 2][:, :], in_=x1_d[ts(dc, 128), :]).then_inc(dma.h, 16)
            if dc >= 1:
                s.wait_ge(dv.h, fin_dv0 + 2 * (dc - 1) + 2)
                s.dma_start(out=fin_o[ts(dc - 1, 128), :],
                            in_=fout[:, ((dc - 1) % 2) * TPC:((dc - 1) % 2) * TPC + TPC]
                            ).then_inc(dma.h, 16)
        s.wait_ge(dv.h, fin_dv0 + 2 * NCH)
        s.dma_start(out=fin_o[ts(NCH - 1, 128), :],
                    in_=fout[:, ((NCH - 1) % 2) * TPC:((NCH - 1) % 2) * TPC + TPC]
                    ).then_inc(dma.h, 16)
        s.wait_ge(dma.h, dma.n + 16 * (NCH + NCH))
    dma.n += 16 * (NCH + NCH)

    pe_f0 = pe.n

    @blk.tensor
    def _(t):
        for dc in range(NCH):
            t.wait_ge(dmg.h, dmg0 + 16 * dc + 16)
            t.wait_ge(dma.h, dma_f0 + 16 * (2 * dc + 1) if dc >= 1 else dma_f0 + 16)
            if dc >= 1:
                t.wait_ge(dv.h, fin_dv0 + 2 * dc)
            for tt in range(2):
                for zc in range(32):
                    t.matmul(ps[4 + tt][:, :], wob[dc % 2][:, zc * 128:zc * 128 + 128],
                             gT[:, zc * TPC + tt * 512:zc * TPC + tt * 512 + 512],
                             start=(zc == 0), stop=False)
                t.matmul(ps[4 + tt][:, :], wu_bf[:, ts(dc, 128)],
                         whT_buf[:, tt * 512:tt * 512 + 512], start=False, stop=False)
                pe.inc(t.matmul(ps[4 + tt][:, :], bu_bf[0:8, ts(dc, 128)],
                                wT_buf[0:8, tt * 512:tt * 512 + 512], start=False, stop=True))

    @blk.vector
    def _(v):
        for dc in range(NCH):
            for tt in range(2):
                v.wait_ge(pe.h, pe_f0 + 2 * dc + tt + 1)
                dv.inc(v.tensor_tensor(out=fout[:, (dc % 2) * TPC + tt * 512:
                                                (dc % 2) * TPC + tt * 512 + 512],
                                       in0=ps[4 + tt][:, :],
                                       in1=x1c[dc % 2][:, tt * 512:tt * 512 + 512],
                                       op=ALU.add))

    @blk.gpsimd
    def _(g):
        g.wait_ge(dmg.h, dmg.n)

    P.end()
    return P


def prep_l1_tail(inp, r3, x1_list):
    nw = _packcol(inp["L1_attn_norm_w"], 32)
    wout = np.ascontiguousarray(np.asarray(inp["L1_attn_W_out"], dtype=np.float32))
    wr = np.ascontiguousarray(np.asarray(inp["L1_ffn_Wr"], dtype=np.float32))
    br = np.ascontiguousarray(np.asarray(inp["L1_ffn_br"], dtype=np.float32)[None, :])
    wd = np.ascontiguousarray(np.asarray(inp["L1_ffn_Wd"], dtype=np.float32)
                              .transpose(1, 0, 2).reshape(D, 128))
    bd = np.ascontiguousarray(np.asarray(inp["L1_ffn_bd"], dtype=np.float32).reshape(1, 128))
    wu = np.ascontiguousarray(np.asarray(inp["L1_ffn_Wu"], dtype=np.float32).reshape(128, D))
    bu = np.ascontiguousarray(np.asarray(inp["L1_ffn_bu"], dtype=np.float32))
    maps = []
    for c in range(8):
        b, g = c // 4, c % 4
        spri = np.zeros((3, 64, 128 * 64), np.float32)
        lpri = np.ones((64, 2), np.float32)
        for k in range(3):
            j = g - 1 - k
            if j >= 0:
                spri[k] = np.asarray(r3[4 * b + j]["spub"]).reshape(64, 128 * 64)
        for k in range(2):
            j = g - 1 - k
            if j >= 0:
                lpri[:, k] = np.asarray(r3[4 * b + j]["lam"])[:, 0]
        maps.append(dict(
            spri=spri, lpri=lpri, cfm=r3[c]["cfm"], expa=r3[c]["expa"],
            ylocal=r3[c]["ylocal"], z=r3[c]["z"], xn1=r3[c]["xn1"], x1=x1_list[c],
            nw=nw, wout=wout, wr=wr, br=br, wdall=wd, bd=bd, wuall=wu, bu=bu))
    return maps
